# version 02 (final)
# baseline (speedup 1.0000x reference)
import math

import numpy as np
import ml_dtypes
import concourse.bass as bass
import concourse.mybir as mybir
from concourse.bass_utils import run_bass_kernel_spmd
from concourse.bass_core_rust import IndirectOffsetOnAxis

F32 = mybir.dt.float32
BF16 = mybir.dt.bfloat16
I32 = mybir.dt.int32
AF = mybir.ActivationFunctionType
ALU = mybir.AluOpType
AX = mybir.AxisListType
BF = ml_dtypes.bfloat16

NCORES = 8
B = 2
L = 4096
D = 2048
CH = 512
TPC = 2 * CH
KD = D // 128
RS = 1.0 / math.sqrt(8192.0)


def core_chunks(c):
    s, i = divmod(c, 4)
    return s, i, 7 - i



def stage_lhsT(wT, msize):
    din, dout = wT.shape
    nk = din // 128
    n_ms = dout // msize
    a = wT.reshape(nk, 128, n_ms, msize)
    a = np.transpose(a, (2, 1, 0, 3))
    return np.ascontiguousarray(a.reshape(n_ms, 128, nk * msize))


def h2perm():
    r = np.arange(8192)
    return (r % 64) * 128 + (r // 64)


def hadamard(n):
    h = np.array([[1.0]], np.float32)
    while h.shape[0] < n:
        h = np.block([[h, h], [h, -h]])
    return h



ENGS = ("sync", "vector", "scalar", "tensor", "gpsimd")


class Ctx:
    def __init__(self, nc, stack):
        self.nc = nc
        self.stack = stack
        self.steps = {e: [] for e in ENGS}
        self.n = 0
        self.sems = {}
        self.consts = {}
        self.regs = {}

    def constf(self, val):
        val = float(val)
        if val in self.consts:
            return
        t = self.stack.enter_context(
            self.nc.sbuf_tensor(f"cstf_{len(self.consts)}", [128, 1], F32))
        self.consts[val] = t
        self.ser("vector", lambda e: e.memset(t[:], val))
        self.nc.const_aps.register(F32, val, t[:])

    def sem(self, name):
        if name not in self.sems:
            self.sems[name] = self.stack.enter_context(self.nc.semaphore(name))
        return self.sems[name]

    def ser(self, eng, fn, dma=False):
        sem = self.sem("CH")
        wait_at = self.n
        inc = 16 if dma else 1
        self.n += inc

        def step(e):
            e.wait_ge(sem, wait_at)
            fn(e).then_inc(sem, inc)

        self.steps[eng].append((step, inc))

    def cid_init(self, cid_sb):
        self._cid_sb = cid_sb
        for eng in ENGS:
            def step_factory(eng):
                def fn(e):
                    reg = e.alloc_register("cidreg_" + eng)
                    self.regs[eng] = reg
                    return e.reg_load(reg, self._cid_sb[0:1, 0:1])
                return fn
            self.ser(eng, step_factory(eng))

    def guarded(self, owner, emit_fn):
        marks = {e: len(self.steps[e]) for e in ENGS}
        n0 = self.n
        emit_fn()
        sem = self.sem("CH")
        for eng in ENGS:
            sub = self.steps[eng][marks[eng]:]
            if not sub:
                continue
            del self.steps[eng][marks[eng]:]
            inc_tot = sum(i for _, i in sub)

            def gstep(e, sub=sub, inc_tot=inc_tot, eng=eng, n0=n0):
                reg = self.regs[eng]
                with e.If_eq(reg, owner):
                    for fn, _ in sub:
                        fn(e)
                with e.If_ne(reg, owner):
                    e.wait_ge(sem, n0)
                    rem = inc_tot
                    while rem > 0:
                        k = min(48, rem)
                        e.nop().then_inc(sem, k)
                        rem -= k
                e.end_ifs()

            self.steps[eng].append((gstep, inc_tot))

    def finish(self, block):
        sem = self.sem("CH")
        final = self.n

        def fin(e):
            e.wait_ge(sem, final)

        for e in ENGS:
            self.steps[e].append((fin, 0))
        block.sync(self._runner("sync"))
        block.vector(self._runner("vector"))
        block.scalar(self._runner("scalar"))
        block.tensor(self._runner("tensor"))
        block.gpsimd(self._runner("gpsimd"))

    def _runner(self, name):
        def body(e):
            for fn, _ in self.steps[name]:
                fn(e)
        return body



def emit_load_x(ctx, xf, x_in):
    for k in range(KD):
        ctx.ser("sync", lambda e, k=k: e.dma_start(
            out=xf[:, k * TPC:(k + 1) * TPC], in_=x_in[k * 128:(k + 1) * 128, :]), dma=True)


def emit_store_x(ctx, xf, x_out):
    for k in range(KD):
        ctx.ser("sync", lambda e, k=k: e.dma_start(
            out=x_out[k * 128:(k + 1) * 128, :], in_=xf[:, k * TPC:(k + 1) * TPC]), dma=True)


def emit_stats(ctx, s, src_fn, T, eps_a, eps_c, rms=False, pfx="r", off=0,
               aux=None):
    pbS, pbQ = s["pb"][6], s["pb"][5]
    ctx.constf(eps_c)
    if aux:
        for _, a2, c2, _ in aux:
            ctx.constf(c2)
    for k in range(KD):
        xsl = src_fn(k)
        xb = s["xb"][:, k * T: k * T + T]
        ctx.ser("vector", lambda e, a=xb, b=xsl: e.tensor_copy(a, b))
        if not rms:
            ctx.ser("tensor", lambda e, a=xb, k=k: e.matmul(
                pbS[0:1, 0:T], s["ones_col"], a, start=(k == 0), stop=(k == KD - 1),
                skip_group_check=True))
        ctx.ser("scalar", lambda e, a=xb: e.activation(s["tmpb"][:, 0:T], a, AF.Square))
        ctx.ser("tensor", lambda e, k=k: e.matmul(
            pbQ[0:1, 0:T], s["ones_col"], s["tmpb"][:, 0:T], start=(k == 0),
            stop=(k == KD - 1), skip_group_check=True))
    inv = 1.0 / D
    o = off

    def finish(pfx2, a2, c2, rms2):
        rstd = s[pfx2 + "_rstd"]
        if rms2:
            ctx.ser("scalar", lambda e: e.activation(
                s["row1"][0:1, o:o + T], pbQ[0:1, 0:T], AF.Ln, scale=inv * a2, bias=c2))
            ctx.ser("scalar", lambda e: e.activation(
                rstd[0:1, o:o + T], s["row1"][0:1, o:o + T], AF.Exp, scale=-0.5))
        else:
            negms = s[pfx2 + "_negms"]
            ctx.ser("scalar", lambda e: e.activation(
                s["row3"][0:1, 0:T], s["row1"][0:1, o:o + T], AF.Ln, scale=a2, bias=c2))
            ctx.ser("scalar", lambda e: e.activation(
                rstd[0:1, o:o + T], s["row3"][0:1, 0:T], AF.Exp, scale=-0.5))
            ctx.ser("vector", lambda e: e.scalar_tensor_tensor(
                negms[0:1, o:o + T], s["row2"][0:1, o:o + T], -1.0, rstd[0:1, o:o + T],
                ALU.mult, ALU.mult))

    if rms:
        finish(pfx, eps_a, eps_c, True)
    else:
        ctx.ser("scalar", lambda e: e.activation(
            s["row2"][0:1, o:o + T], pbS[0:1, 0:T], AF.Copy, scale=inv))
        ctx.ser("scalar", lambda e: e.activation(
            s["row3"][0:1, 0:T], pbS[0:1, 0:T], AF.Square, scale=inv))
        ctx.ser("vector", lambda e: e.scalar_tensor_tensor(
            s["row1"][0:1, o:o + T], pbQ[0:1, 0:T], inv, s["row3"][0:1, 0:T],
            ALU.mult, ALU.subtract))
        finish(pfx, eps_a, eps_c, False)
    if aux:
        for pfx2, a2, c2, rms2 in aux:
            finish(pfx2, a2, c2, rms2)


def emit_bcast(ctx, s, row_ap, bank, T):
    ctx.ser("tensor", lambda e: e.matmul(
        s["pb"][bank][:, 0:T], s["ones_row_f"], row_ap, start=True, stop=True))


def emit_xhat(ctx, s, src_fn, dst_fn, T, rms=False, pfx="r", off=0,
              gscale=None, f32_dst=None):
    emit_bcast(ctx, s, s[pfx + "_rstd"][0:1, off:off + T], 4, T)
    if not rms:
        emit_bcast(ctx, s, s[pfx + "_negms"][0:1, off:off + T], 3, T)
    for k in range(KD):
        xsl = src_fn(k)
        dst = dst_fn(k)
        fdst = None if f32_dst is None else f32_dst(k)
        steps = []
        if rms:
            pre = fdst if (fdst is not None) else (s["sc"][:, 0:T] if gscale is not None else dst)
            ctx.ser("vector", lambda e, a=xsl, o=pre: e.tensor_tensor(
                o, a, s["pb"][4][:, 0:T], ALU.mult))
        else:
            ctx.ser("vector", lambda e, a=xsl: e.tensor_tensor(
                s["sc"][:, 0:T], a, s["pb"][4][:, 0:T], ALU.mult))
            pre = fdst if (fdst is not None) else (s["sc"][:, 0:T] if gscale is not None else dst)
            ctx.ser("vector", lambda e, o=pre: e.tensor_tensor(
                o, s["sc"][:, 0:T], s["pb"][3][:, 0:T], ALU.add))
        if gscale is not None:
            ctx.ser("scalar", lambda e, a=pre, o=dst, k=k: e.activation(
                o, a, AF.Copy, scale=gscale[:, k:k + 1]))
        elif fdst is not None:
            ctx.ser("vector", lambda e, a=fdst, o=dst: e.tensor_copy(o, a))



def emit_l0_ffn_v2h(ctx, s, wd):
    v = wd["v2h"].rearrange("(a b) d -> b a d", b=128)
    for cb in range(64):
        par = cb % 2
        wsl = s["wbuf"][:, par * 8192: par * 8192 + 2048]
        ctx.ser("gpsimd", lambda e, cb=cb, w=wsl: e.dma_start(
            out=w, in_=wd["w2s"][cb * 128:(cb + 1) * 128, :]), dma=True)
        for q in range(4):
            bank = s["pb"][q % 2]
            ctx.ser("tensor", lambda e, o=bank, w=wsl, q=q: e.matmul(
                o[:, 0:CH], s["T2"], w[:, q * CH:(q + 1) * CH], start=True, stop=True))
            ctx.ser("scalar", lambda e, o=bank, q=q: e.activation(
                s["xb2"][:, q * CH:(q + 1) * CH], o[:, 0:CH], AF.Copy))
        for blo in range(2):
            ctx.ser("sync", lambda e, cb=cb, blo=blo: e.dma_start(
                out=v[2 * cb + blo], in_=s["xb2"][blo * 64: blo * 64 + 64, 0:2048]), dma=True)


def emit_l0_ffn(ctx, s, xf, wd):
    hidp = s["flexA"]
    for tt in range(2):
        t0 = tt * CH
        src = lambda k, t0=t0: xf[:, k * TPC + t0: k * TPC + t0 + CH]
        dst = lambda k: s["xb"][:, k * CH: (k + 1) * CH]
        emit_stats(ctx, s, src, CH, 1.0, 1e-5)
        emit_xhat(ctx, s, src, dst, CH)

        for half in range(2):
            for msl in range(8):
                ms = half * 8 + msl
                for wkey, par in (("w1", 0), ("w3", 1)):
                    wsl = s["wbuf"][:, par * 8192: par * 8192 + 8192]
                    ctx.ser("gpsimd", lambda e, ms=ms, w=wsl, wk=wkey: e.dma_start(
                        out=w, in_=wd[wk][ms]), dma=True)
                for mci in range(4):
                    mc = msl * 4 + mci
                    for wkey, par, bank in (("w1", 0, 4), ("w3", 1, 5)):
                        wsl = s["wbuf"][:, par * 8192: par * 8192 + 8192]
                        pbk = s["pb"][bank]
                        for k in range(16):
                            lhs = wsl[:, k * 512 + mci * 128: k * 512 + mci * 128 + 128]
                            ctx.ser("tensor", lambda e, o=pbk, l=lhs, k=k: e.matmul(
                                o[:, 0:CH], l, s["xb"][:, k * CH:(k + 1) * CH],
                                start=(k == 0), stop=(k == 15)))
                    ctx.ser("scalar", lambda e: e.activation(
                        s["sc"][:, 0:CH], s["pb"][4][:, 0:CH], AF.Silu))
                    ctx.ser("vector", lambda e: e.tensor_tensor(
                        s["tmpb"][:, 0:CH], s["sc"][:, 0:CH], s["pb"][5][:, 0:CH], ALU.mult))
                    ctx.ser("tensor", lambda e: e.matmul(
                        s["pb"][6][:, 0:CH], s["H128"], s["tmpb"][:, 0:CH],
                        start=True, stop=True))
                    ctx.ser("scalar", lambda e, mc=mc: e.activation(
                        hidp[:, mc * CH:(mc + 1) * CH], s["pb"][6][:, 0:CH],
                        AF.Copy, scale=RS))
            for dg in range(4):
                for hki in range(32):
                    hk = half * 32 + hki
                    par = hki % 2
                    vt = s["wbuf"][:, par * 8192: par * 8192 + 512]
                    src = wd["v2h"][hk * 128:(hk + 1) * 128, dg * 512:(dg + 1) * 512]
                    ctx.ser("gpsimd", lambda e, v=vt, r=src: e.dma_start(out=v, in_=r), dma=True)
                    for mci in range(4):
                        ctx.ser("tensor", lambda e, o=s["pb"][mci], v=vt, mci=mci,
                                hki=hki: e.matmul(
                            o[:, 0:CH], v[:, mci * 128: mci * 128 + 128],
                            hidp[:, hki * CH:(hki + 1) * CH],
                            start=(hki == 0), stop=(hki == 31),
                            skip_group_check=True))
                for mci in range(4):
                    mc = dg * 4 + mci
                    xsl = xf[:, mc * TPC + t0: mc * TPC + t0 + CH]
                    ctx.ser("vector", lambda e, x=xsl, o=s["pb"][mci]: e.tensor_tensor(
                        x, x, o[:, 0:CH], ALU.add))



def emit_quant_weights(ctx, s, wd):
    for wn in ("wq", "wk", "wv", "wo"):
        src, dst = wd[wn + "s"], wd["q" + wn]
        for hs in range(16):
            sl = src[hs // 2][:, (hs % 2) * 2048:(hs % 2) * 2048 + 2048]
            ctx.ser("sync", lambda e, sl=sl: e.dma_start(
                out=s["flexF"][:], in_=sl), dma=True)
            ctx.ser("scalar", lambda e, hs=hs: e.activation(
                s["wbuf"][:, 0:2048], s["flexF"][:], AF.Abs,
                accum_out=s["sc"][:, hs:hs + 1]))
        ctx.ser("vector", lambda e: e.reduce_sum(s["sc2"][:, 0:1], s["sc"][:, 0:16], axis=AX.X))
        ctx.ser("tensor", lambda e: e.matmul(
            s["pb"][6][0:1, 0:1], s["ones_col_f"], s["sc2"][:, 0:1], start=True, stop=True))
        ctx.ser("scalar", lambda e: e.activation(
            s["row3"][0:1, 0:1], s["pb"][6][0:1, 0:1], AF.Copy, scale=1.0 / (D * D)))
        ctx.ser("vector", lambda e: e.tensor_scalar_max(
            s["row3"][0:1, 0:1], s["row3"][0:1, 0:1], 1e-5))
        ctx.ser("tensor", lambda e: e.matmul(
            s["pb"][3][:, 0:1], s["ones_row_f"], s["row3"][0:1, 0:1], start=True, stop=True))
        ctx.ser("scalar", lambda e, wn=wn: e.activation(
            s[wn + "_sh"][:], s["pb"][3][:, 0:1], AF.Copy, scale=0.5))
        ctx.ser("scalar", lambda e, wn=wn: e.activation(
            s[wn + "_nh"][:], s["pb"][3][:, 0:1], AF.Copy, scale=-0.5))
        for hs in range(16):
            sl = src[hs // 2][:, (hs % 2) * 2048:(hs % 2) * 2048 + 2048]
            dsl = dst[hs // 2][:, (hs % 2) * 2048:(hs % 2) * 2048 + 2048]
            ctx.ser("sync", lambda e, sl=sl: e.dma_start(
                out=s["flexF"][:], in_=sl), dma=True)
            ctx.ser("scalar", lambda e, wn=wn: e.activation(
                s["wbuf"][:, 0:2048], s["flexF"][:], AF.Sign, bias=s[wn + "_nh"][:, 0:1]))
            ctx.ser("scalar", lambda e, wn=wn: e.activation(
                s["wbuf"][:, 2048:4096], s["flexF"][:], AF.Sign, bias=s[wn + "_sh"][:, 0:1]))
            ctx.ser("vector", lambda e: e.tensor_tensor(
                s["wbuf"][:, 4096:6144], s["wbuf"][:, 0:2048], s["wbuf"][:, 2048:4096],
                ALU.add))
            ctx.ser("sync", lambda e, dsl=dsl: e.dma_start(
                out=dsl, in_=s["wbuf"][:, 4096:6144]), dma=True)
    ctx.ser("scalar", lambda e: e.activation(
        s["wq_se"][:], s["wq_sh"][:], AF.Copy, scale=1.0 / math.sqrt(128.0)))


DBG_TAPS = {}
DBG_TAPS_REQ = {}
LAST_RESULTS = None


def emit_l0_attn_a(ctx, s, xf, wd):
    emit_quant_weights(ctx, s, wd)
    EXT = 640
    for tt in range(2):
        t0 = tt * CH
        def srch(k, tt=tt):
            ctx.ser("sync", lambda e, k=k: e.dma_start(
                out=s["sc2"][:, 0:128],
                in_=wd["x_halo"][k * 128:(k + 1) * 128, tt * 128:(tt + 1) * 128]), dma=True)
            return s["sc2"][:, 0:128]
        srco = lambda k, t0=t0: xf[:, k * TPC + t0: k * TPC + t0 + CH]
        emit_stats(ctx, s, srch, 128, 1.0, 1e-5, off=0)
        emit_stats(ctx, s, srco, CH, 1.0, 1e-5, off=128,
                   aux=[("q", 1.0 + 1e-6, 1e-11, False)])
        emit_xhat(ctx, s, srch, lambda k: s["xb"][:, k * EXT: k * EXT + 128], 128, off=0)
        emit_xhat(ctx, s, srco, lambda k: s["xb"][:, k * EXT + 128: k * EXT + EXT],
                  CH, off=128)
        emit_xhat(ctx, s, srco, lambda k: s["xb2"][:, k * CH: (k + 1) * CH],
                  CH, pfx="q", off=128, gscale=s["g_sb"])

        for ms in range(8):
            ctx.ser("sync", lambda e, ms=ms: e.dma_start(
                out=s["wbuf"][:, 0:4096], in_=wd["qwv"][ms]), dma=True)
            for tc in range(5):
                for k in range(KD):
                    lhs = s["xb"][:, k * EXT + tc * 128: k * EXT + tc * 128 + 128]
                    rhs = s["wbuf"][:, k * 256:(k + 1) * 256]
                    ctx.ser("tensor", lambda e, l=lhs, r=rhs, k=k: e.matmul(
                        s["pb"][4][:, 0:256], l, r, start=(k == 0), stop=(k == KD - 1)))
                ctx.ser("scalar", lambda e, tc=tc, ms=ms: e.activation(
                    s["flexA"][:, tc * 2048 + ms * 256: tc * 2048 + ms * 256 + 256],
                    s["pb"][4][:, 0:256], AF.Copy, scale=s["wv_sh"][:, 0:1]))

        PS = 10240
        for h in range(16):
            ms, half = divmod(h, 2)
            if half == 0:
                ctx.ser("sync", lambda e, ms=ms: e.dma_start(
                    out=s["wbuf"][:, 0:4096], in_=wd["qwk"][ms]), dma=True)
                ctx.ser("sync", lambda e, ms=ms: e.dma_start(
                    out=s["wbuf"][:, 4096:8192], in_=wd["qwq"][ms]), dma=True)
            for kp, kw in ((0, 512), (512, 128)):
                for k in range(KD):
                    lhs = s["wbuf"][:, k * 256 + half * 128: k * 256 + half * 128 + 128]
                    rhs = s["xb"][:, k * EXT + kp: k * EXT + kp + kw]
                    ctx.ser("tensor", lambda e, l=lhs, r=rhs, k=k, kw=kw: e.matmul(
                        s["pb"][4][:, 0:kw], l, r, start=(k == 0), stop=(k == KD - 1)))
                ctx.ser("scalar", lambda e, kp=kp, kw=kw: e.activation(
                    s["khqh"][:, kp:kp + kw], s["pb"][4][:, 0:kw], AF.Copy,
                    scale=s["wk_sh"][:, 0:1]))
            for k in range(KD):
                lhs = s["wbuf"][:, 4096 + k * 256 + half * 128: 4096 + k * 256 + half * 128 + 128]
                rhs = s["xb2"][:, k * CH:(k + 1) * CH]
                ctx.ser("tensor", lambda e, l=lhs, r=rhs, k=k: e.matmul(
                    s["pb"][4][:, 0:CH], l, r, start=(k == 0), stop=(k == KD - 1)))
            ctx.ser("scalar", lambda e: e.activation(
                s["khqh"][:, 640:1152], s["pb"][4][:, 0:CH], AF.Copy,
                scale=s["wq_se"][:, 0:1]))
            ctx.ser("vector", lambda e: e.tensor_tensor(
                s["tmpb"][:, 0:CH], s["khqh"][:, 128:640], s["khqh"][:, 640:1152], ALU.mult))
            ctx.ser("tensor", lambda e: e.matmul(
                s["pb"][5][0:1, 0:CH], s["ones_col"], s["tmpb"][:, 0:CH],
                start=True, stop=True))
            ctx.ser("scalar", lambda e: e.activation(
                s["rowb"][0:1, 0:CH], s["pb"][5][0:1, 0:CH], AF.Copy, scale=-1.0))
            for kc in range(5):
                ctx.ser("tensor", lambda e, kc=kc: e.matmul(
                    s["pb"][0][:, 0:CH], s["khqh"][:, kc * 128:(kc + 1) * 128],
                    s["khqh"][:, 640:1152], start=True, stop=False,
                    skip_group_check=True))
                ctx.ser("tensor", lambda e: e.matmul(
                    s["pb"][0][:, 0:CH], s["ones_row_b"], s["rowb"][0:1, 0:CH],
                    start=False, stop=True, skip_group_check=True))
                ctx.ser("vector", lambda e, kc=kc: e.tensor_tensor(
                    s["pb"][0][:, 0:CH], s["pb"][0][:, 0:CH],
                    s["wmask"][:, kc * CH:(kc + 1) * CH], ALU.add))
                if kc == 0:
                    ctx.ser("vector", lambda e, tt=tt: e.tensor_single_scalar(
                        s["pb"][0][:, 0:CH], s["pb"][0][:, 0:CH],
                        s["kill_sb"][:, tt:tt + 1], ALU.add))
                ctx.ser("scalar", lambda e, kc=kc: e.activation(
                    s["flexA"][:, PS + kc * CH: PS + (kc + 1) * CH],
                    s["pb"][0][:, 0:CH], AF.Exp))
                ctx.ser("tensor", lambda e, kc=kc: e.matmul(
                    s["pb"][2][0:1, 0:CH], s["ones_col"],
                    s["flexA"][:, PS + kc * CH: PS + (kc + 1) * CH],
                    start=(kc == 0), stop=(kc == 4), skip_group_check=True))
                ctx.ser("tensor", lambda e, kc=kc, h=h: e.matmul(
                    s["pb"][1][:, 0:CH],
                    s["flexA"][:, kc * 2048 + h * 128: kc * 2048 + h * 128 + 128],
                    s["flexA"][:, PS + kc * CH: PS + (kc + 1) * CH],
                    start=(kc == 0), stop=(kc == 4), skip_group_check=True))
            if tt == 0 and h == 0 and "ph0" in DBG_TAPS:
                ctx.ser("sync", lambda e: e.dma_start(
                    out=DBG_TAPS["ph0"], in_=s["flexA"][:, PS:PS + 5 * CH]), dma=True)
                ctx.ser("scalar", lambda e: e.activation(
                    s["sc"][:, 0:CH], s["pb"][1][:, 0:CH], AF.Copy))
                ctx.ser("sync", lambda e: e.dma_start(
                    out=DBG_TAPS["ctxr0"], in_=s["sc"][:, 0:CH]), dma=True)
                ctx.ser("scalar", lambda e: e.activation(
                    s["sc"][0:1, 0:CH], s["pb"][2][0:1, 0:CH], AF.Copy))
                ctx.ser("sync", lambda e: e.dma_start(
                    out=DBG_TAPS["l0"], in_=s["sc"][0:1, 0:CH]), dma=True)
            ctx.ser("vector", lambda e: e.reciprocal(
                s["row3"][0:1, 0:CH], s["pb"][2][0:1, 0:CH]))
            ctx.ser("tensor", lambda e: e.matmul(
                s["pb"][3][:, 0:CH], s["ones_row_f"], s["row3"][0:1, 0:CH],
                start=True, stop=True))
            ctx.ser("scalar", lambda e: e.activation(
                s["sc"][:, 0:CH], s["pb"][3][:, 0:CH], AF.Copy))
            ctx.ser("vector", lambda e, h=h: e.tensor_tensor(
                s["wbuf"][:, 8192 + h * CH: 8192 + (h + 1) * CH],
                s["pb"][1][:, 0:CH], s["sc"][:, 0:CH], ALU.mult))

        if tt == 0 and "ctx_all" in DBG_TAPS:
            ctx.ser("sync", lambda e: e.dma_start(
                out=DBG_TAPS["ctx_all"], in_=s["wbuf"][:, 8192:16384]), dma=True)
        for ms in range(8):
            ctx.ser("sync", lambda e, ms=ms: e.dma_start(
                out=s["wbuf"][:, 0:4096], in_=wd["qwo"][ms]), dma=True)
            for mci in range(2):
                mc = ms * 2 + mci
                for k in range(KD):
                    lhs = s["wbuf"][:, k * 256 + mci * 128: k * 256 + mci * 128 + 128]
                    rhs = s["wbuf"][:, 8192 + k * CH: 8192 + (k + 1) * CH]
                    ctx.ser("tensor", lambda e, l=lhs, r=rhs, k=k: e.matmul(
                        s["pb"][0][:, 0:CH], l, r, start=(k == 0), stop=(k == KD - 1)))
                xsl = xf[:, mc * TPC + t0: mc * TPC + t0 + CH]
                ctx.ser("vector", lambda e, x=xsl: e.scalar_tensor_tensor(
                    x, s["pb"][0][:, 0:CH], s["wo_sh"][:, 0:1], x, ALU.mult, ALU.add))
                xhsl = s["xb"][:, mc * 640 + 128: mc * 640 + 640]
                ctx.ser("vector", lambda e, x=xsl, xh=xhsl: e.tensor_tensor(
                    x, x, xh, ALU.add))



def own_chunk(g):
    return g if g < 4 else 7 - g


def tslot_chunk(g):
    return 0 if g < 4 else 1


def emit_l0_attn_b(ctx, s, xf, wd, my_group_rank_of):
    nc = ctx.nc
    NQ = 4

    for tt in range(2):
        t0 = tt * CH
        src = lambda k, t0=t0: xf[:, k * TPC + t0: k * TPC + t0 + CH]
        dstb = lambda k: s["xb"][:, k * CH: (k + 1) * CH]
        emit_stats(ctx, s, src, CH, 1.0, 1e-5)
        emit_xhat(ctx, s, src, dstb, CH)
        for ms in range(22):
            par = ms % 2
            wsl = s["wbuf"][:, par * 8192: par * 8192 + 6144]
            ctx.ser("gpsimd", lambda e, ms=ms, w=wsl: e.dma_start(
                out=w, in_=wd["wins"][ms]), dma=True)
            for mci in range(3):
                mc = ms * 3 + mci
                bank = s["pb"][mci % 2]
                for k in range(KD):
                    lhs = wsl[:, k * 384 + mci * 128: k * 384 + mci * 128 + 128]
                    ctx.ser("tensor", lambda e, o=bank, l=lhs, k=k: e.matmul(
                        o[:, 0:CH], l, s["xb"][:, k * CH:(k + 1) * CH],
                        start=(k == 0), stop=(k == KD - 1)))
                ctx.ser("scalar", lambda e, o=bank, mc=mc: e.activation(
                    s["tmpb"][:, 0:CH], o[:, 0:CH], AF.Copy,
                    scale=s["mup_sb"][:, mc:mc + 1]))
                if mc < 32:
                    ctx.ser("sync", lambda e, mc=mc, t0=t0: e.dma_start(
                        out=wd["z_dram"][mc * 128:(mc + 1) * 128, t0:t0 + CH],
                        in_=s["tmpb"][:, 0:CH]), dma=True)
                else:
                    cx = mc - 32
                    ctx.ser("sync", lambda e, cx=cx, tt=tt: e.dma_start(
                        out=wd["xbc_dram"][tt][cx * 128:(cx + 1) * 128, :],
                        in_=s["tmpb"][:, 0:CH]), dma=True)
                    ctx.ser("vector", lambda e, cx=cx, tt=tt: e.tensor_copy(
                        s["bndst"][:, tt * 102 + cx * 3: tt * 102 + cx * 3 + 3],
                        s["tmpb"][:, CH - 3:CH]))
        ctx.ser("gpsimd", lambda e: e.dma_start(
            out=s["wbuf"][:, 0:1024], in_=wd["windt"]), dma=True)
        for k in range(KD):
            lhs = s["wbuf"][:, k * 64:(k + 1) * 64]
            ctx.ser("tensor", lambda e, l=lhs, k=k: e.matmul(
                s["pb"][2][0:64, 0:CH], l, s["xb"][:, k * CH:(k + 1) * CH],
                start=(k == 0), stop=(k == KD - 1)))
        ctx.ser("scalar", lambda e: e.activation(
            s["sc"][0:64, 0:CH], s["pb"][2][0:64, 0:CH], AF.Copy,
            scale=s["mup_sb"][0:64, 66:67]))
        ctx.ser("vector", lambda e: e.tensor_scalar_add(
            s["sc"][0:64, 0:CH], s["sc"][0:64, 0:CH], s["dtb_sb"][0:64, 0:1]))
        ctx.ser("scalar", lambda e: e.activation(
            s["sc2"][0:64, 0:CH], s["sc"][0:64, 0:CH], AF.Abs))
        ctx.ser("scalar", lambda e: e.activation(
            s["sc2"][0:64, 0:CH], s["sc2"][0:64, 0:CH], AF.Exp, scale=-1.0))
        ctx.constf(1.0)
        ctx.ser("scalar", lambda e: e.activation(
            s["sc2"][0:64, 0:CH], s["sc2"][0:64, 0:CH], AF.Ln, bias=1.0))
        ctx.ser("vector", lambda e: e.tensor_scalar_max(
            s["dts"][0:64, 0:CH], s["sc"][0:64, 0:CH], 0.0))
        ctx.ser("vector", lambda e: e.tensor_add(
            s["dts"][0:64, 0:CH], s["dts"][0:64, 0:CH], s["sc2"][0:64, 0:CH]))
        ctx.ser("vector", lambda e: e.tensor_scalar(
            s["sc"][0:64, 0:CH], s["dts"][0:64, 0:CH], s["ea_sb"][0:64, 0:1], -1.0,
            ALU.mult, ALU.mult))
        ctx.ser("vector", lambda e: e.memset(s["ssb"][0:64, 0:1], 0.0))
        ctx.ser("vector", lambda e: e.tensor_tensor_scan(
            s["ssb"][0:64, 1:513], s["gts"][0:64, 0:CH], s["sc"][0:64, 0:CH],
            0.0, ALU.mult, ALU.add))
        ctx.ser("sync", lambda e, tt=tt: e.dma_start(
            out=wd["dts_dram"][tt], in_=s["dts"][0:64, 0:CH]), dma=True)
        ctx.ser("sync", lambda e, tt=tt: e.dma_start(
            out=wd["ssb_dram"][tt], in_=s["ssb"][0:64, 0:513]), dma=True)
        for q in range(5):
            r = tt * 5 + q
            ctx.ser("sync", lambda e, r=r, q=q: e.dma_start(
                out=wd["sbase_dram"][r:r + 1, :].rearrange("a b -> b a"),
                in_=s["ssb"][0:64, q * 128: q * 128 + 1]), dma=True)

    ctx.ser("sync", lambda e: e.dma_start(
        out=wd["cc_bnd"][0:128, :], in_=s["bndst"][:, 0:102]), dma=True)
    ctx.ser("sync", lambda e: e.dma_start(
        out=wd["cc_bnd"][128:256, :], in_=s["bndst"][:, 102:204]), dma=True)
    ctx.ser("vector", lambda e: e.memset(s["hstage"][:], 0.0))
    ctx.ser("sync", lambda e: e.dma_start(
        out=wd["bnd_gat"][1024:1152, :], in_=s["hstage"][:]), dma=True)
    ctx.ser("gpsimd", lambda e: e.collective_compute(
        "AllGather", ALU.bypass, [[0, 1, 2, 3], [4, 5, 6, 7]],
        [wd["cc_bnd"].opt()], [wd["bnd_gat"][0:1024, :].opt()]))

    for tt in range(2):
        ctx.ser("gpsimd", lambda e, tt=tt: e.indirect_dma_start(
            s["hstage"][:], None, wd["bnd_gat"],
            IndirectOffsetOnAxis(s["bidx_sb"][:, tt:tt + 1], 0)), dma=True)
        if "hstage" in DBG_TAPS and tt == 1:
            ctx.ser("sync", lambda e: e.dma_start(
                out=DBG_TAPS["hstage"], in_=s["hstage"][:]), dma=True)
        if "bnd_gat" in DBG_TAPS and tt == 1:
            ctx.ser("sync", lambda e: e.dma_start(
                out=DBG_TAPS["bnd_gat"], in_=wd["bnd_gat"][0:1024, :]), dma=True)
        ctx.ser("sync", lambda e, tt=tt: e.dma_start(
            out=s["dts"][0:64, 0:CH], in_=wd["dts_dram"][tt]), dma=True)
        ctx.ser("sync", lambda e, tt=tt: e.dma_start(
            out=s["ssb"][0:64, 0:513], in_=wd["ssb_dram"][tt]), dma=True)
        for cx in range(34):
            ctx.ser("sync", lambda e, cx=cx, tt=tt: e.dma_start(
                out=s["tmpb"][:, 0:CH],
                in_=wd["xbc_dram"][tt][cx * 128:(cx + 1) * 128, :]), dma=True)
            ctx.ser("vector", lambda e, cx=cx: e.tensor_copy(
                s["cvt"][:, 0:3], s["hstage"][:, cx * 3: cx * 3 + 3]))
            ctx.ser("vector", lambda e: e.tensor_copy(
                s["cvt"][:, 3:515], s["tmpb"][:, 0:CH]))
            ctx.ser("vector", lambda e, cx=cx: e.tensor_scalar_mul(
                s["sc"][:, 0:CH], s["cvt"][:, 0:CH],
                s["convw_sb"][:, cx * 4: cx * 4 + 1]))
            for j in (1, 2, 3):
                ctx.ser("vector", lambda e, cx=cx, j=j: e.scalar_tensor_tensor(
                    s["sc"][:, 0:CH], s["cvt"][:, j:j + CH],
                    s["convw_sb"][:, cx * 4 + j: cx * 4 + j + 1],
                    s["sc"][:, 0:CH], ALU.mult, ALU.add))
            ctx.ser("scalar", lambda e, cx=cx: e.activation(
                s["sc2"][:, 0:CH], s["sc"][:, 0:CH], AF.Silu,
                bias=s["convb_sb"][:, cx:cx + 1]))
            if cx < 32:
                ctx.ser("vector", lambda e: e.tensor_copy(
                    s["tmpb"][:, 0:CH], s["sc2"][:, 0:CH]))
                for q in range(NQ):
                    ctx.ser("tensor", lambda e, q=q: e.transpose(
                        s["pbt"][:, 0:128], s["tmpb"][:, q * 128:(q + 1) * 128],
                        s["idb"]))
                    ctx.ser("scalar", lambda e, q=q, cx=cx: e.activation(
                        s["flexA"][:, q * 4096 + cx * 128: q * 4096 + cx * 128 + 128],
                        s["pbt"][:, 0:128], AF.Copy))
            else:
                ctx.ser("vector", lambda e, cx=cx: e.tensor_copy(
                    s["khqh"][:, (cx - 32) * CH: (cx - 31) * CH], s["sc2"][:, 0:CH]))
        ctx.ser("sync", lambda e, tt=tt: e.dma_start(
            out=wd["bc_dram"][tt], in_=s["khqh"][:, 0:1024]), dma=True)
        for q in range(NQ):
            ctx.ser("tensor", lambda e, q=q: e.matmul(
                s["pb"][2][:, 0:128], s["khqh"][:, q * 128:(q + 1) * 128],
                s["khqh"][:, 512 + q * 128: 512 + (q + 1) * 128],
                start=True, stop=True))
            ctx.ser("scalar", lambda e, q=q: e.activation(
                s["gts"][:, q * 128:(q + 1) * 128], s["pb"][2][:, 0:128], AF.Copy))
            ctx.ser("tensor", lambda e, q=q: e.transpose(
                s["pbt"][:, 0:128], s["khqh"][:, q * 128:(q + 1) * 128], s["idb"]))
            ctx.ser("scalar", lambda e, q=q: e.activation(
                s["btm"][:, q * 128:(q + 1) * 128], s["pbt"][:, 0:128], AF.Copy))
            ctx.ser("tensor", lambda e, q=q: e.transpose(
                s["pb"][3][0:128, 0:64], s["dts"][0:64, q * 128:(q + 1) * 128],
                s["idf"][0:64, 0:64]))
            ctx.ser("scalar", lambda e, q=q: e.activation(
                s["dttm"][:, q * 64:(q + 1) * 64], s["pb"][3][0:128, 0:64], AF.Copy))
            ctx.ser("sync", lambda e, tt=tt, q=q: e.dma_start(
                out=s["rq64"][0:1, 0:64], in_=wd["sbase_dram"][tt * 5 + q:tt * 5 + q + 1, :]), dma=True)
            ctx.ser("sync", lambda e, tt=tt, q=q: e.dma_start(
                out=s["rq64"][0:1, 64:128], in_=wd["sbase_dram"][tt * 5 + q + 1:tt * 5 + q + 2, :]), dma=True)
            ctx.ser("tensor", lambda e: e.matmul(
                s["pb"][3][:, 0:64], s["ones_row_f"], s["rq64"][0:1, 0:64],
                start=True, stop=True))
            ctx.ser("tensor", lambda e, q=q: e.transpose(
                s["pb"][4][0:128, 0:64], s["ssb"][0:64, 1 + q * 128: 1 + (q + 1) * 128],
                s["idf"][0:64, 0:64]))
            ctx.ser("scalar", lambda e: e.activation(
                s["sc2"][:, 0:64], s["pb"][3][:, 0:64], AF.Copy))
            ctx.ser("vector", lambda e: e.tensor_sub(
                s["sc"][:, 0:64], s["pb"][4][0:128, 0:64], s["sc2"][:, 0:64]))
            ctx.ser("scalar", lambda e, q=q: e.activation(
                s["rql"][:, q * 64:(q + 1) * 64], s["sc"][:, 0:64], AF.Exp))
            ctx.ser("vector", lambda e: e.tensor_sub(
                s["rq64"][0:1, 64:128], s["rq64"][0:1, 64:128], s["rq64"][0:1, 0:64]))
            ctx.ser("scalar", lambda e: e.activation(
                s["rq64"][0:1, 64:128], s["rq64"][0:1, 64:128], AF.Exp))
            ctx.ser("tensor", lambda e: e.matmul(
                s["pb"][3][:, 0:64], s["ones_row_f"], s["rq64"][0:1, 64:128],
                start=True, stop=True))
            ctx.ser("scalar", lambda e, q=q: e.activation(
                s["decqb"][:, q * 64:(q + 1) * 64], s["pb"][3][:, 0:64], AF.Copy))
        ctx.ser("vector", lambda e: e.memset(s["state_sb"][:], 0.0))
        for h in range(64):
            ctx.ser("sync", lambda e, tt=tt, h=h: e.dma_start(
                out=s["srow"][0:1, 0:513], in_=wd["ssb_dram"][tt][h:h + 1, :]), dma=True)
            ctx.ser("scalar", lambda e: e.activation(
                s["srowneg"][0:1, 0:513], s["srow"][0:1, 0:513], AF.Copy, scale=-1.0))
            for q in range(NQ):
                ctx.ser("tensor", lambda e, q=q: e.matmul(
                    s["pb"][4][:, 0:128], s["ones_row_f"],
                    s["srow"][0:1, 1 + q * 128: 1 + (q + 1) * 128],
                    start=True, stop=False, skip_group_check=True))
                ctx.ser("tensor", lambda e, q=q: e.matmul(
                    s["pb"][4][:, 0:128],
                    s["srowneg"][0:1, 1 + q * 128: 1 + (q + 1) * 128],
                    s["ones_row_f"], start=False, stop=True, skip_group_check=True))
                ctx.ser("vector", lambda e: e.tensor_add(
                    s["pb"][4][:, 0:128], s["pb"][4][:, 0:128], s["tri_sb"][:, 0:128]))
                ctx.ser("scalar", lambda e: e.activation(
                    s["sc"][:, 0:128], s["pb"][4][:, 0:128], AF.Exp))
                ctx.ser("vector", lambda e, q=q, h=h: e.tensor_scalar_mul(
                    s["sc"][:, 0:128], s["sc"][:, 0:128],
                    s["dttm"][:, q * 64 + h: q * 64 + h + 1]))
                ctx.ser("vector", lambda e, q=q: e.tensor_mul(
                    s["tmpb"][:, 0:128], s["sc"][:, 0:128],
                    s["gts"][:, q * 128:(q + 1) * 128]))
                ctx.ser("tensor", lambda e, q=q, h=h: e.matmul(
                    s["pb"][5][:, 0:64], s["tmpb"][:, 0:128],
                    s["flexA"][:, q * 4096 + h * 64: q * 4096 + h * 64 + 64],
                    start=True, stop=True))
                ctx.ser("vector", lambda e, q=q: e.tensor_scalar_mul(
                    s["tmpb"][:, 128:256], s["btm"][:, q * 128:(q + 1) * 128],
                    s["sc"][:, 127:128]))
                ctx.ser("tensor", lambda e, q=q, h=h: e.matmul(
                    s["pb"][6][:, 0:64], s["tmpb"][:, 128:256],
                    s["flexA"][:, q * 4096 + h * 64: q * 4096 + h * 64 + 64],
                    start=True, stop=True))
                ctx.ser("tensor", lambda e, q=q, h=h: e.matmul(
                    s["pb"][2][:, 0:64], s["khqh"][:, 512 + q * 128: 512 + (q + 1) * 128],
                    s["state_sb"][:, h * 64:(h + 1) * 64], start=True, stop=True))
                ctx.ser("vector", lambda e, q=q, h=h: e.scalar_tensor_tensor(
                    s["state_sb"][:, h * 64:(h + 1) * 64],
                    s["state_sb"][:, h * 64:(h + 1) * 64],
                    s["decqb"][:, q * 64 + h: q * 64 + h + 1],
                    s["pb"][6][:, 0:64], ALU.mult, ALU.add))
                ctx.ser("vector", lambda e, q=q, h=h: e.scalar_tensor_tensor(
                    s["flexA"][:, q * 4096 + h * 64: q * 4096 + h * 64 + 64],
                    s["flexA"][:, q * 4096 + h * 64: q * 4096 + h * 64 + 64],
                    s["Dbc_sb"][:, h:h + 1], s["pb"][5][:, 0:64],
                    ALU.mult, ALU.add))
                ctx.ser("vector", lambda e, q=q, h=h: e.scalar_tensor_tensor(
                    s["flexA"][:, q * 4096 + h * 64: q * 4096 + h * 64 + 64],
                    s["pb"][2][:, 0:64], s["rql"][:, q * 64 + h: q * 64 + h + 1],
                    s["flexA"][:, q * 4096 + h * 64: q * 4096 + h * 64 + 64],
                    ALU.mult, ALU.add))
        ctx.ser("sync", lambda e, tt=tt: e.dma_start(
            out=wd["L_dram"][tt], in_=s["state_sb"][:]), dma=True)
        for q in range(NQ):
            ctx.ser("sync", lambda e, tt=tt, q=q: e.dma_start(
                out=wd["ytm_dram"][tt][q], in_=s["flexA"][:, q * 4096:(q + 1) * 4096]),
                dma=True)

    ctx.ser("vector", lambda e: e.memset(s["state_sb"][:], 0.0))
    for t in range(2):
        ctx.ser("sync", lambda e, t=t: e.dma_start(
            out=wd["stin_dram"][t], in_=s["state_sb"][:]), dma=True)
    for bnd in range(8):
        for grp in range(2):
            owner = grp * 4 + own_chunk(bnd)
            t = tslot_chunk(bnd)

            def body(owner=owner, t=t, bnd=bnd):
                ctx.ser("sync", lambda e, t=t: e.dma_start(
                    out=s["flexA"][:, 0:4096], in_=wd["L_dram"][t]), dma=True)
                ctx.ser("sync", lambda e, t=t: e.dma_start(
                    out=s["rq64"][0:1, 0:64], in_=wd["sbase_dram"][t * 5 + 4:t * 5 + 5, :]), dma=True)
                ctx.ser("scalar", lambda e: e.activation(
                    s["rq64"][0:1, 0:64], s["rq64"][0:1, 0:64], AF.Exp))
                ctx.ser("tensor", lambda e: e.matmul(
                    s["pb"][3][:, 0:64], s["ones_row_f"], s["rq64"][0:1, 0:64],
                    start=True, stop=True))
                ctx.ser("scalar", lambda e: e.activation(
                    s["decb"][:, 0:64], s["pb"][3][:, 0:64], AF.Copy))
                for h in range(64):
                    ctx.ser("vector", lambda e, h=h: e.scalar_tensor_tensor(
                        s["state_sb"][:, h * 64:(h + 1) * 64],
                        s["state_sb"][:, h * 64:(h + 1) * 64],
                        s["decb"][:, h:h + 1],
                        s["flexA"][:, h * 64:(h + 1) * 64], ALU.mult, ALU.add))
                ctx.ser("sync", lambda e: e.dma_start(
                    out=wd["cc_st"], in_=s["state_sb"][:]), dma=True)

            ctx.guarded(owner, body)
        if bnd == 7:
            break
        ctx.ser("gpsimd", lambda e: e.collective_compute(
            "AllGather", ALU.bypass, [[0, 1, 2, 3], [4, 5, 6, 7]],
            [wd["cc_st"].opt()], [wd["st_gat"].opt()]))
        for grp in range(2):
            nxt = grp * 4 + own_chunk(bnd + 1)
            src_rank = own_chunk(bnd)
            t = tslot_chunk(bnd + 1)

            def rbody(src_rank=src_rank, t=t):
                ctx.ser("sync", lambda e, src_rank=src_rank: e.dma_start(
                    out=s["state_sb"][:],
                    in_=wd["st_gat"][src_rank * 128:(src_rank + 1) * 128, :]), dma=True)
                ctx.ser("sync", lambda e, t=t: e.dma_start(
                    out=wd["stin_dram"][t], in_=s["state_sb"][:]), dma=True)

            ctx.guarded(nxt, rbody)

    for tt in range(2):
        ctx.ser("sync", lambda e, tt=tt: e.dma_start(
            out=s["state_sb"][:], in_=wd["stin_dram"][tt]), dma=True)
        if "stin" in DBG_TAPS and tt == 1:
            ctx.ser("sync", lambda e: e.dma_start(
                out=DBG_TAPS["stin"], in_=s["state_sb"][:]), dma=True)
        ctx.ser("sync", lambda e, tt=tt: e.dma_start(
            out=s["khqh"][:, 0:1024], in_=wd["bc_dram"][tt]), dma=True)
        ctx.ser("sync", lambda e, tt=tt: e.dma_start(
            out=s["ssb"][0:64, 0:513], in_=wd["ssb_dram"][tt]), dma=True)
        for q in range(NQ):
            ctx.ser("sync", lambda e, tt=tt, q=q: e.dma_start(
                out=s["flexA"][:, 0:4096], in_=wd["ytm_dram"][tt][q]), dma=True)
            ctx.ser("sync", lambda e, tt=tt, q=q: e.dma_start(
                out=s["rq64"][0:1, 0:64], in_=wd["sbase_dram"][tt * 5 + q:tt * 5 + q + 1, :]), dma=True)
            ctx.ser("tensor", lambda e: e.matmul(
                s["pb"][3][:, 0:64], s["ones_row_f"], s["rq64"][0:1, 0:64],
                start=True, stop=True))
            ctx.ser("scalar", lambda e: e.activation(
                s["sc2"][:, 0:64], s["pb"][3][:, 0:64], AF.Copy))
            ctx.ser("tensor", lambda e, q=q: e.transpose(
                s["pb"][4][0:128, 0:64], s["ssb"][0:64, 1 + q * 128: 1 + (q + 1) * 128],
                s["idf"][0:64, 0:64]))
            ctx.ser("vector", lambda e: e.tensor_sub(
                s["sc"][:, 0:64], s["pb"][4][0:128, 0:64], s["sc2"][:, 0:64]))
            ctx.ser("scalar", lambda e: e.activation(
                s["rql"][:, 0:64], s["sc"][:, 0:64], AF.Exp))
            ctx.ser("scalar", lambda e: e.activation(
                s["rq64"][0:1, 0:64], s["rq64"][0:1, 0:64], AF.Exp))
            ctx.ser("tensor", lambda e: e.matmul(
                s["pb"][3][:, 0:64], s["ones_row_f"], s["rq64"][0:1, 0:64],
                start=True, stop=True))
            ctx.ser("scalar", lambda e: e.activation(
                s["decb"][:, 0:64], s["pb"][3][:, 0:64], AF.Copy))
            for h in range(64):
                ctx.ser("vector", lambda e, h=h: e.tensor_scalar_mul(
                    s["flexA"][:, 4096 + h * 64: 4096 + (h + 1) * 64],
                    s["state_sb"][:, h * 64:(h + 1) * 64], s["decb"][:, h:h + 1]))
            for sl in range(8):
                bank = s["pb"][sl % 2]
                ctx.ser("tensor", lambda e, q=q, sl=sl, o=bank: e.matmul(
                    o[:, 0:CH], s["khqh"][:, 512 + q * 128: 512 + (q + 1) * 128],
                    s["flexA"][:, 4096 + sl * 512: 4096 + (sl + 1) * 512],
                    start=True, stop=True))
                for hi in range(8):
                    h = sl * 8 + hi
                    ctx.ser("vector", lambda e, h=h, hi=hi, o=bank: e.scalar_tensor_tensor(
                        s["flexA"][:, h * 64:(h + 1) * 64], o[:, hi * 64:(hi + 1) * 64],
                        s["rql"][:, h:h + 1], s["flexA"][:, h * 64:(h + 1) * 64],
                        ALU.mult, ALU.add))
            ctx.ser("sync", lambda e, tt=tt, q=q: e.dma_start(
                out=wd["ytm_dram"][tt][q], in_=s["flexA"][:, 0:4096]), dma=True)

    ctx.constf(1e-5)
    for tt in range(2):
        t0 = tt * CH
        for cx in range(32):
            for q in range(NQ):
                ctx.ser("sync", lambda e, tt=tt, q=q, cx=cx: e.dma_start(
                    out=s["tmpb"][:, 0:128],
                    in_=wd["ytm_dram"][tt][q][:, cx * 128:(cx + 1) * 128]), dma=True)
                ctx.ser("tensor", lambda e: e.transpose(
                    s["pbt"][:, 0:128], s["tmpb"][:, 0:128], s["idb"]))
                ctx.ser("scalar", lambda e, q=q: e.activation(
                    s["tmpb"][:, 128 + q * 128: 256 + q * 128], s["pbt"][:, 0:128],
                    AF.Copy))
            ctx.ser("sync", lambda e, cx=cx, t0=t0: e.dma_start(
                out=s["khqh"][:, 0:CH],
                in_=wd["z_dram"][cx * 128:(cx + 1) * 128, t0:t0 + CH]), dma=True)
            ctx.ser("scalar", lambda e: e.activation(
                s["sc"][:, 0:CH], s["khqh"][:, 0:CH], AF.Silu))
            ctx.ser("vector", lambda e: e.tensor_mul(
                s["sc"][:, 0:CH], s["sc"][:, 0:CH], s["tmpb"][:, 128:640]))
            ctx.ser("scalar", lambda e: e.activation(
                s["sc2"][:, 0:CH], s["sc"][:, 0:CH], AF.Square))
            ctx.ser("tensor", lambda e, cx=cx: e.matmul(
                s["pb"][6][0:1, 0:CH], s["ones_col_f"], s["sc2"][:, 0:CH],
                start=(cx == 0), stop=(cx == 31), skip_group_check=True))
            ctx.ser("scalar", lambda e, cx=cx: e.activation(
                s["wbuf"][:, cx * CH:(cx + 1) * CH], s["sc"][:, 0:CH], AF.Copy,
                scale=s["nw_sb"][:, cx:cx + 1]))
        ctx.ser("scalar", lambda e: e.activation(
            s["row1"][0:1, 0:CH], s["pb"][6][0:1, 0:CH], AF.Ln,
            scale=1.0 / 4096.0, bias=1e-5))
        ctx.ser("scalar", lambda e: e.activation(
            s["row1"][0:1, 0:CH], s["row1"][0:1, 0:CH], AF.Exp, scale=-0.5))
        ctx.ser("tensor", lambda e: e.matmul(
            s["pb"][3][:, 0:CH], s["ones_row_f"], s["row1"][0:1, 0:CH],
            start=True, stop=True))
        ctx.ser("scalar", lambda e: e.activation(
            s["sc2"][:, 0:CH], s["pb"][3][:, 0:CH], AF.Copy))
        for ms in range(8):
            ctx.ser("gpsimd", lambda e, ms=ms: e.dma_start(
                out=s["xb"][:, 0:8192], in_=wd["wouts"][ms]), dma=True)
            for mci in range(2):
                mc = ms * 2 + mci
                bank = s["pb"][4 + mci]
                for k in range(32):
                    lhs = s["xb"][:, k * 256 + mci * 128: k * 256 + mci * 128 + 128]
                    ctx.ser("tensor", lambda e, o=bank, l=lhs, k=k: e.matmul(
                        o[:, 0:CH], l, s["wbuf"][:, k * CH:(k + 1) * CH],
                        start=(k == 0), stop=(k == 31)))
                xsl = xf[:, mc * TPC + t0: mc * TPC + t0 + CH]
                ctx.ser("vector", lambda e, o=bank: e.tensor_mul(
                    s["sc"][:, 0:CH], o[:, 0:CH], s["sc2"][:, 0:CH]))
                ctx.ser("vector", lambda e, x=xsl: e.tensor_add(x, x, s["sc"][:, 0:CH]))



def emit_l1_attn_a(ctx, s, xf, wd):
    nc = ctx.nc
    V = lambda i: s["l1av_sb"][:, i * 16:(i + 1) * 16]
    N1W, N1B, N2W, N2B, TD, FAA, MR, MK, MV, CM = range(10)
    ctx.ser("scalar", lambda e: e.activation(s["etd"][:], V(TD), AF.Exp))
    ctx.ser("scalar", lambda e: e.activation(s["wcol"][:], s["etd"][:], AF.Exp, scale=-1.0))
    ctx.ser("scalar", lambda e: e.activation(s["efa"][:], V(FAA), AF.Exp))
    ctx.ser("vector", lambda e: e.tensor_scalar_mul(s["lw"][:], s["etd"][:], -512.0))
    ctx.ser("vector", lambda e: e.memset(s["onesT2"][:], 1.0))

    for tt in range(2):
        t0 = tt * CH
        src = lambda k, t0=t0: xf[:, k * TPC + t0: k * TPC + t0 + CH]
        emit_stats(ctx, s, src, CH, 1.00001, 1.00001e-5, pfx="r")
        emit_bcast(ctx, s, s["r_rstd"][0:1, 0:CH], 4, CH)
        emit_bcast(ctx, s, s["r_negms"][0:1, 0:CH], 3, CH)
        for k in range(KD):
            xsl = src(k)
            ctx.ser("vector", lambda e, a=xsl: e.tensor_tensor(
                s["sc"][:, 0:CH], a, s["pb"][4][:, 0:CH], ALU.mult))
            ctx.ser("vector", lambda e: e.tensor_tensor(
                s["sc"][:, 0:CH], s["sc"][:, 0:CH], s["pb"][3][:, 0:CH], ALU.add))
            ctx.ser("scalar", lambda e, k=k: e.activation(
                s["sc2"][:, 0:CH], s["sc"][:, 0:CH], AF.Copy, scale=V(N1W)[:, k:k + 1]))
            ctx.ser("vector", lambda e, k=k: e.tensor_scalar_add(
                s["sc2"][:, 0:CH], s["sc2"][:, 0:CH], V(N1B)[:, k:k + 1]))
            ctx.ser("vector", lambda e, k=k, tt=tt: e.tensor_copy(
                s["flexA"][:, tt * 8192 + k * CH: tt * 8192 + (k + 1) * CH],
                s["sc2"][:, 0:CH]))
            ctx.ser("vector", lambda e, k=k, tt=tt: e.tensor_copy(
                s["st16"][:, tt * 16 + k: tt * 16 + k + 1], s["sc2"][:, CH - 1:CH]))
    ctx.ser("sync", lambda e: e.dma_start(out=wd["cc16"][0:128, :], in_=s["st16"][:, 0:16]), dma=True)
    ctx.ser("sync", lambda e: e.dma_start(out=wd["cc16"][128:256, :], in_=s["st16"][:, 16:32]), dma=True)
    ctx.ser("vector", lambda e: e.memset(s["hh"][:], 0.0))
    ctx.ser("sync", lambda e: e.dma_start(out=wd["gat16"][1024:1152, :], in_=s["hh"][:]), dma=True)
    ctx.ser("gpsimd", lambda e: e.collective_compute(
        "AllGather", ALU.bypass, [[0, 1, 2, 3], [4, 5, 6, 7]],
        [wd["cc16"].opt()], [wd["gat16"][0:1024, :].opt()]))

    for tt in range(2):
        half = tt * 8192
        ctx.ser("gpsimd", lambda e, tt=tt: e.indirect_dma_start(
            s["hh"][:], None, wd["gat16"],
            IndirectOffsetOnAxis(s["bidx2_sb"][:, tt:tt + 1], 0)), dma=True)
        for k in range(KD):
            hsl = s["flexA"][:, half + k * CH: half + (k + 1) * CH]
            ctx.ser("vector", lambda e, h=hsl: e.tensor_copy(s["sc"][:, 1:CH], h[:, 0:CH - 1]))
            ctx.ser("vector", lambda e, k=k: e.tensor_copy(s["sc"][:, 0:1], s["hh"][:, k:k + 1]))
            ctx.ser("vector", lambda e, h=hsl: e.tensor_tensor(
                s["sc2"][:, 0:CH], h, s["sc"][:, 0:CH], ALU.subtract))
            for mx, dst in ((MR, s["xb"]), (MK, s["xb2"]), (MV, None)):
                dstsl = (s["wbuf"][:, 8192 + k * CH: 8192 + (k + 1) * CH] if dst is None
                         else dst[:, k * CH:(k + 1) * CH])
                ctx.ser("vector", lambda e, mx=mx, k=k, o=dstsl: e.scalar_tensor_tensor(
                    o, s["sc2"][:, 0:CH], V(mx)[:, k:k + 1], s["sc"][:, 0:CH],
                    ALU.mult, ALU.add))
        for wkey, rhsbuf, evac in (("wr", "xb", "rsig"), ("wk", "xb2k", "kb"),
                                   ("wv", "wbufhi", "kv")):
            for ms in range(4):
                ctx.ser("gpsimd", lambda e, wkey=wkey, ms=ms: e.dma_start(
                    out=s["wbuf"][:, 0:8192], in_=wd[wkey][ms]), dma=True)
                for mci in range(4):
                    mc = ms * 4 + mci
                    bank = s["pb"][mci % 2]
                    for k in range(KD):
                        lhs = s["wbuf"][:, k * 512 + mci * 128: k * 512 + mci * 128 + 128]
                        if rhsbuf == "xb":
                            rhs = s["xb"][:, k * CH:(k + 1) * CH]
                        elif rhsbuf == "xb2k":
                            rhs = s["xb2"][:, k * CH:(k + 1) * CH]
                        else:
                            rhs = s["wbuf"][:, 8192 + k * CH: 8192 + (k + 1) * CH]
                        ctx.ser("tensor", lambda e, o=bank, l=lhs, r=rhs, k=k: e.matmul(
                            o[:, 0:CH], l, r, start=(k == 0), stop=(k == KD - 1)))
                    if evac == "rsig":
                        ctx.ser("scalar", lambda e, o=bank, mc=mc, half=half: e.activation(
                            s["flexA"][:, half + mc * CH: half + (mc + 1) * CH],
                            o[:, 0:CH], AF.Sigmoid))
                    elif evac == "kb":
                        ctx.ser("scalar", lambda e, o=bank, mc=mc: e.activation(
                            s["xb"][:, mc * CH:(mc + 1) * CH], o[:, 0:CH], AF.Copy))
                    else:
                        ctx.ser("vector", lambda e, o=bank, mc=mc: e.tensor_tensor(
                            s["xb2"][:, mc * CH:(mc + 1) * CH],
                            s["xb"][:, mc * CH:(mc + 1) * CH], o[:, 0:CH], ALU.mult))
        for k in range(KD):
            ctx.ser("sync", lambda e, tt=tt, k=k: e.dma_start(
                out=wd["kv_dram"][tt][k * 128:(k + 1) * 128, :],
                in_=s["xb2"][:, k * CH:(k + 1) * CH]), dma=True)
        for k in range(KD):
            ctx.ser("vector", lambda e, k=k: e.tensor_scalar_mul(
                s["sc"][:, 0:CH], s["onesT2"][:], s["wcol"][:, k:k + 1]))
            ctx.ser("vector", lambda e, k=k: e.tensor_tensor_scan(
                s["sc2"][:, 0:CH], s["sc"][:, 0:CH], s["xb2"][:, k * CH:(k + 1) * CH],
                0.0, ALU.mult, ALU.add))
            ctx.ser("vector", lambda e, k=k, tt=tt: e.tensor_copy(
                s["st16"][:, tt * 16 + k: tt * 16 + k + 1], s["sc2"][:, CH - 1:CH]))
    ctx.ser("sync", lambda e: e.dma_start(out=wd["cc16"][0:128, :], in_=s["st16"][:, 0:16]), dma=True)
    ctx.ser("sync", lambda e: e.dma_start(out=wd["cc16"][128:256, :], in_=s["st16"][:, 16:32]), dma=True)
    ctx.ser("gpsimd", lambda e: e.collective_compute(
        "AllGather", ALU.bypass, [[0, 1, 2, 3], [4, 5, 6, 7]],
        [wd["cc16"].opt()], [wd["gat16"][0:1024, :].opt()]))

    for tt in range(2):
        t0 = tt * CH
        half = tt * 8192
        for j in range(8):
            rb = own_chunk(j) * 256 + tslot_chunk(j) * 128
            ctx.ser("sync", lambda e, j=j, rb=rb: e.dma_start(
                out=s["Lg"][:, j * 16:(j + 1) * 16], in_=wd["gat16"][rb:rb + 128, :]), dma=True)
        for k in range(KD):
            ctx.ser("sync", lambda e, tt=tt, k=k: e.dma_start(
                out=s["xb2"][:, k * CH:(k + 1) * CH],
                in_=wd["kv_dram"][tt][k * 128:(k + 1) * 128, :]), dma=True)
        ctx.ser("vector", lambda e: e.memset(s["carry"][:], 0.0))
        for j in range(8):
            ctx.ser("scalar", lambda e, j=j, tt=tt: e.activation(
                s["wj"][:], s["lw"][:], AF.Copy, scale=s["mfac_sb"][:, tt * 8 + j: tt * 8 + j + 1]))
            ctx.ser("scalar", lambda e: e.activation(s["wj"][:], s["wj"][:], AF.Exp))
            ctx.ser("vector", lambda e, j=j: e.tensor_tensor(
                s["wj"][:], s["wj"][:], s["Lg"][:, j * 16:(j + 1) * 16], ALU.mult))
            ctx.ser("vector", lambda e: e.tensor_add(s["carry"][:], s["carry"][:], s["wj"][:]))
        src = lambda k, t0=t0: xf[:, k * TPC + t0: k * TPC + t0 + CH]
        emit_stats(ctx, s, src, CH, 1.0, 1e-5, pfx="r")
        for k in range(KD):
            ctx.ser("vector", lambda e, k=k: e.tensor_scalar_mul(
                s["sc"][:, 0:CH], s["onesT2"][:], s["wcol"][:, k:k + 1]))
            ctx.ser("vector", lambda e, k=k: e.tensor_tensor_scan(
                s["sc2"][:, 0:CH], s["sc"][:, 0:CH], s["xb2"][:, k * CH:(k + 1) * CH],
                s["carry"][:, k:k + 1], ALU.mult, ALU.add))
            ctx.ser("vector", lambda e, k=k: e.tensor_scalar_mul(
                s["colA"][:], s["xb2"][:, k * CH: k * CH + 1], s["efa"][:, k:k + 1]))
            ctx.ser("vector", lambda e, tt=tt: e.tensor_scalar_mul(
                s["colA"][:], s["colA"][:], s["fam_sb"][:, tt:tt + 1]))
            ctx.ser("vector", lambda e: e.tensor_add(
                s["sc2"][:, 0:1], s["sc2"][:, 0:1], s["colA"][:]))
            ctx.ser("vector", lambda e, k=k, half=half: e.tensor_tensor(
                s["wbuf"][:, 8192 + k * CH: 8192 + (k + 1) * CH],
                s["flexA"][:, half + k * CH: half + (k + 1) * CH],
                s["sc2"][:, 0:CH], ALU.mult))
        emit_bcast(ctx, s, s["r_rstd"][0:1, 0:CH], 4, CH)
        emit_bcast(ctx, s, s["r_negms"][0:1, 0:CH], 3, CH)
        for ms in range(4):
            ctx.ser("gpsimd", lambda e, ms=ms: e.dma_start(
                out=s["wbuf"][:, 0:8192], in_=wd["wo2"][ms]), dma=True)
            for mci in range(4):
                mc = ms * 4 + mci
                bank = s["pb"][mci % 2]
                for k in range(KD):
                    lhs = s["wbuf"][:, k * 512 + mci * 128: k * 512 + mci * 128 + 128]
                    rhs = s["wbuf"][:, 8192 + k * CH: 8192 + (k + 1) * CH]
                    ctx.ser("tensor", lambda e, o=bank, l=lhs, r=rhs, k=k: e.matmul(
                        o[:, 0:CH], l, r, start=(k == 0), stop=(k == KD - 1)))
                xsl = xf[:, mc * TPC + t0: mc * TPC + t0 + CH]
                ctx.ser("vector", lambda e, a=xsl: e.tensor_tensor(
                    s["sc"][:, 0:CH], a, s["pb"][4][:, 0:CH], ALU.mult))
                ctx.ser("vector", lambda e: e.tensor_tensor(
                    s["sc"][:, 0:CH], s["sc"][:, 0:CH], s["pb"][3][:, 0:CH], ALU.add))
                ctx.ser("vector", lambda e, o=bank: e.tensor_tensor(
                    s["sc2"][:, 0:CH], s["sc"][:, 0:CH], o[:, 0:CH], ALU.add))
                ctx.ser("sync", lambda e, tt=tt, mc=mc: e.dma_start(
                    out=wd["xmid_dram"][tt][mc * 128:(mc + 1) * 128, :],
                    in_=s["sc2"][:, 0:CH]), dma=True)
                ctx.ser("vector", lambda e, mc=mc: e.tensor_copy(
                    s["xml"][:, mc:mc + 1], s["sc2"][:, CH - 1:CH]))
        emit_stats(ctx, s, lambda k: s["xml"][:, k:k + 1], 1, 1.0, 1e-5, pfx="r")
        ctx.ser("tensor", lambda e: e.matmul(
            s["pb"][3][:, 0:1], s["ones_row_f"], s["r_rstd"][0:1, 0:1], start=True, stop=True))
        ctx.ser("scalar", lambda e: e.activation(s["colA"][:], s["pb"][3][:, 0:1], AF.Copy))
        ctx.ser("tensor", lambda e: e.matmul(
            s["pb"][3][:, 0:1], s["ones_row_f"], s["r_negms"][0:1, 0:1], start=True, stop=True))
        ctx.ser("scalar", lambda e: e.activation(s["colB"][:], s["pb"][3][:, 0:1], AF.Copy))
        for k in range(KD):
            ctx.ser("vector", lambda e, k=k: e.tensor_scalar_mul(
                s["wj"][:, 0:1], s["xml"][:, k:k + 1], s["colA"][:, 0:1]))
            ctx.ser("vector", lambda e: e.tensor_add(
                s["wj"][:, 0:1], s["wj"][:, 0:1], s["colB"][:]))
            ctx.ser("vector", lambda e, k=k: e.tensor_scalar_mul(
                s["wj"][:, 0:1], s["wj"][:, 0:1], V(N2W)[:, k:k + 1]))
            ctx.ser("vector", lambda e, k=k: e.tensor_scalar_add(
                s["wj"][:, 0:1], s["wj"][:, 0:1], V(N2B)[:, k:k + 1]))
            ctx.ser("vector", lambda e, k=k, tt=tt: e.tensor_copy(
                s["st16"][:, tt * 16 + k: tt * 16 + k + 1], s["wj"][:, 0:1]))
    ctx.ser("sync", lambda e: e.dma_start(out=wd["cc16"][0:128, :], in_=s["st16"][:, 0:16]), dma=True)
    ctx.ser("sync", lambda e: e.dma_start(out=wd["cc16"][128:256, :], in_=s["st16"][:, 16:32]), dma=True)
    ctx.ser("gpsimd", lambda e: e.collective_compute(
        "AllGather", ALU.bypass, [[0, 1, 2, 3], [4, 5, 6, 7]],
        [wd["cc16"].opt()], [wd["gat16"][0:1024, :].opt()]))

    for tt in range(2):
        t0 = tt * CH
        ctx.ser("gpsimd", lambda e, tt=tt: e.indirect_dma_start(
            s["hh"][:], None, wd["gat16"],
            IndirectOffsetOnAxis(s["bidx2_sb"][:, tt:tt + 1], 0)), dma=True)

        def srcm(k, tt=tt):
            ctx.ser("sync", lambda e, k=k, tt=tt: e.dma_start(
                out=s["cvt2"][:, 0:CH],
                in_=wd["xmid_dram"][tt][k * 128:(k + 1) * 128, :]), dma=True)
            return s["cvt2"][:, 0:CH]

        emit_stats(ctx, s, srcm, CH, 1.0, 1e-5, pfx="r")
        emit_bcast(ctx, s, s["r_rstd"][0:1, 0:CH], 4, CH)
        emit_bcast(ctx, s, s["r_negms"][0:1, 0:CH], 3, CH)
        for k in range(KD):
            ctx.ser("sync", lambda e, k=k, tt=tt: e.dma_start(
                out=s["cvt2"][:, 0:CH],
                in_=wd["xmid_dram"][tt][k * 128:(k + 1) * 128, :]), dma=True)
            ctx.ser("vector", lambda e: e.tensor_tensor(
                s["sc2"][:, 0:CH], s["cvt2"][:, 0:CH], s["pb"][4][:, 0:CH], ALU.mult))
            ctx.ser("vector", lambda e: e.tensor_tensor(
                s["sc2"][:, 0:CH], s["sc2"][:, 0:CH], s["pb"][3][:, 0:CH], ALU.add))
            ctx.ser("scalar", lambda e, k=k: e.activation(
                s["sc2"][:, 0:CH], s["sc2"][:, 0:CH], AF.Copy, scale=V(N2W)[:, k:k + 1]))
            ctx.ser("vector", lambda e, k=k: e.tensor_scalar_add(
                s["sc2"][:, 0:CH], s["sc2"][:, 0:CH], V(N2B)[:, k:k + 1]))
            ctx.ser("vector", lambda e: e.tensor_copy(s["sc"][:, 1:CH], s["sc2"][:, 0:CH - 1]))
            ctx.ser("vector", lambda e, k=k: e.tensor_copy(s["sc"][:, 0:1], s["hh"][:, k:k + 1]))
            ctx.ser("vector", lambda e: e.tensor_tensor(
                s["sc2"][:, 0:CH], s["sc2"][:, 0:CH], s["sc"][:, 0:CH], ALU.subtract))
            ctx.ser("vector", lambda e, k=k: e.scalar_tensor_tensor(
                s["xb"][:, k * CH:(k + 1) * CH], s["sc2"][:, 0:CH], V(CM)[:, k:k + 1],
                s["sc"][:, 0:CH], ALU.mult, ALU.add))
        for ms in range(4):
            ctx.ser("gpsimd", lambda e, ms=ms: e.dma_start(
                out=s["wbuf"][:, 0:8192], in_=wd["wcr"][ms]), dma=True)
            for mci in range(4):
                mc = ms * 4 + mci
                bank = s["pb"][mci % 2]
                for k in range(KD):
                    lhs = s["wbuf"][:, k * 512 + mci * 128: k * 512 + mci * 128 + 128]
                    ctx.ser("tensor", lambda e, o=bank, l=lhs, k=k: e.matmul(
                        o[:, 0:CH], l, s["xb"][:, k * CH:(k + 1) * CH],
                        start=(k == 0), stop=(k == KD - 1)))
                ctx.ser("scalar", lambda e, o=bank, mc=mc: e.activation(
                    s["xb2"][:, mc * CH:(mc + 1) * CH], o[:, 0:CH], AF.Sigmoid))
        for ms in range(12):
            ctx.ser("gpsimd", lambda e, ms=ms: e.dma_start(
                out=s["wbuf"][:, 0:8192], in_=wd["wck"][ms]), dma=True)
            for mci in range(4):
                mc = ms * 4 + mci
                bank = s["pb"][mci % 2]
                for k in range(KD):
                    lhs = s["wbuf"][:, k * 512 + mci * 128: k * 512 + mci * 128 + 128]
                    ctx.ser("tensor", lambda e, o=bank, l=lhs, k=k: e.matmul(
                        o[:, 0:CH], l, s["xb"][:, k * CH:(k + 1) * CH],
                        start=(k == 0), stop=(k == KD - 1)))
                dst = (s["flexA"][:, mc * CH:(mc + 1) * CH] if mc < 32
                       else s["wbuf"][:, 8192 + (mc - 32) * CH: 8192 + (mc - 31) * CH])
                ctx.ser("scalar", lambda e, o=bank, d=dst: e.activation(
                    d, o[:, 0:CH], AF.Square))
        for mc in range(16):
            ctx.ser("gpsimd", lambda e, mc=mc: e.dma_start(
                out=s["wbuf"][:, 0:6144], in_=wd["wcv"][mc]), dma=True)
            bank = s["pb"][mc % 2]
            for k in range(48):
                lhs = s["wbuf"][:, k * 128:(k + 1) * 128]
                rhs = (s["flexA"][:, k * CH:(k + 1) * CH] if k < 32
                       else s["wbuf"][:, 8192 + (k - 32) * CH: 8192 + (k - 31) * CH])
                ctx.ser("tensor", lambda e, o=bank, l=lhs, r=rhs, k=k: e.matmul(
                    o[:, 0:CH], l, r, start=(k == 0), stop=(k == 47)))
            xsl = xf[:, mc * TPC + t0: mc * TPC + t0 + CH]
            ctx.ser("vector", lambda e, o=bank, mc=mc: e.tensor_tensor(
                s["sc"][:, 0:CH], o[:, 0:CH], s["xb2"][:, mc * CH:(mc + 1) * CH], ALU.mult))
            ctx.ser("sync", lambda e, tt=tt, mc=mc: e.dma_start(
                out=s["cvt2"][:, 0:CH],
                in_=wd["xmid_dram"][tt][mc * 128:(mc + 1) * 128, :]), dma=True)
            ctx.ser("vector", lambda e, x=xsl: e.tensor_add(x, x, s["sc"][:, 0:CH]))
            ctx.ser("vector", lambda e, x=xsl: e.tensor_add(x, x, s["cvt2"][:, 0:CH]))



def emit_l1_ffn(ctx, s, xf, wd):
    nc = ctx.nc
    ctx.ser("gpsimd", lambda e: e.dma_start(out=s["ws_sb"][0:16, :], in_=wd["wksp_in"]), dma=True)

    for it in range(2):
        final = (it == 1)
        for tt in range(2):
            t0 = tt * CH
            if it == 0:
                src = lambda k, t0=t0: xf[:, k * TPC + t0: k * TPC + t0 + CH]
                emit_stats(ctx, s, src, CH, 1.0, 1e-5, pfx="r",
                           aux=[("q", 1.000001, 1e-11, False)])
                emit_xhat(ctx, s, src, lambda k: s["xb"][:, k * CH:(k + 1) * CH], CH)
                if tt == 0 and "xhat0" in DBG_TAPS:
                    ctx.ser("sync", lambda e: e.dma_start(
                        out=DBG_TAPS["xhat0"], in_=s["xb"][:, 0:KD * CH]), dma=True)
                ctx.ser("vector", lambda e: e.reciprocal(
                    s["row3"][0:1, 0:CH], s["r_rstd"][0:1, 0:CH]))
                ctx.ser("vector", lambda e: e.tensor_mul(
                    s["row3"][0:1, 0:CH], s["row3"][0:1, 0:CH], s["q_rstd"][0:1, 0:CH]))
                emit_bcast(ctx, s, s["row3"][0:1, 0:CH], 4, CH)
            else:
                def src(k, tt=tt):
                    ctx.ser("sync", lambda e, k=k, tt=tt: e.dma_start(
                        out=s["cvt2"][:, 0:CH],
                        in_=wd["xcur_dram"][tt][k * 128:(k + 1) * 128, :]), dma=True)
                    return s["cvt2"][:, 0:CH]
                emit_stats(ctx, s, src, CH, 1.0, 1e-6, rms=True, pfx="q")
                for k in range(KD):
                    ctx.ser("sync", lambda e, k=k, tt=tt: e.dma_start(
                        out=s["cvt2"][:, 0:CH],
                        in_=wd["xcur_dram"][tt][k * 128:(k + 1) * 128, :]), dma=True)
                    ctx.ser("vector", lambda e, k=k: e.tensor_copy(
                        s["xb"][:, k * CH:(k + 1) * CH], s["cvt2"][:, 0:CH]))
                emit_bcast(ctx, s, s["q_rstd"][0:1, 0:CH], 4, CH)
            ctx.ser("gpsimd", lambda e: e.dma_start(
                out=s["wbuf"][:, 0:256], in_=wd["cws"][0]), dma=True)
            for k in range(KD):
                lhs = s["wbuf"][:, k * 16:(k + 1) * 16]
                ctx.ser("tensor", lambda e, l=lhs, k=k: e.matmul(
                    s["pb"][2][0:16, 0:CH], l, s["xb"][:, k * CH:(k + 1) * CH],
                    start=(k == 0), stop=(k == KD - 1)))
            ctx.ser("scalar", lambda e: e.activation(
                s["sc2"][0:16, 0:CH], s["pb"][4][0:16, 0:CH], AF.Copy))
            ctx.ser("vector", lambda e: e.tensor_tensor(
                s["sc"][0:16, 0:CH], s["pb"][2][0:16, 0:CH], s["sc2"][0:16, 0:CH],
                ALU.mult))
            ctx.ser("vector", lambda e: e.tensor_scalar_add(
                s["sc"][0:16, 0:CH], s["sc"][0:16, 0:CH], s["cb_sb"][0:16, 0:1]))
            if it == 0 and tt == 0 and "lg_t0" in DBG_TAPS:
                ctx.ser("sync", lambda e: e.dma_start(
                    out=DBG_TAPS["lg_t0"], in_=s["sc"][0:16, 0:CH]), dma=True)
            ctx.ser("sync", lambda e, tt=tt: e.dma_start(
                out=wd["cc_lg"][tt * 16:(tt + 1) * 16, :], in_=s["sc"][0:16, 0:CH]), dma=True)
            ctx.ser("vector", lambda e, tt=tt: e.tensor_copy(
                s["lgstb"][0:16, tt * CH:(tt + 1) * CH], s["sc"][0:16, 0:CH]))
            for sl in range(4):
                ctx.ser("gpsimd", lambda e, sl=sl: e.dma_start(
                    out=s["wbuf"][:, 0:8192], in_=wd["wws"][sl]), dma=True)
                ctx.ser("sync", lambda e, sl=sl: e.dma_start(
                    out=s["row3"][0:1, 0:CH], in_=wd["wb_in"][0:1, sl * CH:(sl + 1) * CH]), dma=True)
                ctx.ser("tensor", lambda e: e.matmul(
                    s["pb"][5][:, 0:CH], s["ones_row_f"], s["row3"][0:1, 0:CH],
                    start=True, stop=True))
                ctx.ser("scalar", lambda e: e.activation(
                    s["sc2"][:, 0:CH], s["pb"][5][:, 0:CH], AF.Copy))
                for lc in range(4):
                    bank = s["pb"][lc % 2]
                    for k in range(KD):
                        lhs = s["xb"][:, k * CH + lc * 128: k * CH + lc * 128 + 128]
                        rhs = s["wbuf"][:, k * 512:(k + 1) * 512]
                        ctx.ser("tensor", lambda e, o=bank, l=lhs, r=rhs, k=k: e.matmul(
                            o[:, 0:CH], l, r, start=(k == 0), stop=(k == KD - 1)))
                    lci = tt * 4 + lc
                    ctx.ser("vector", lambda e, o=bank, lci=lci, sl=sl: e.tensor_tensor(
                        s["flexA"][:, lci * 2048 + sl * CH: lci * 2048 + (sl + 1) * CH],
                        o[:, 0:CH], s["sc2"][:, 0:CH], ALU.add))
            if it == 0 and tt == 0 and "wx0" in DBG_TAPS:
                ctx.ser("sync", lambda e: e.dma_start(
                    out=DBG_TAPS["wx0"], in_=s["flexA"][:, 0:2048]), dma=True)
        ctx.ser("gpsimd", lambda e: e.collective_compute(
            "AllGather", ALU.bypass, [[0, 1, 2, 3], [4, 5, 6, 7]],
            [wd["cc_lg"].opt()], [wd["lgat"].opt()]))
        for g in range(8):
            rb = own_chunk(g) * 32 + tslot_chunk(g) * 16
            ctx.ser("gpsimd", lambda e, g=g, rb=rb: e.dma_start(
                out=s["lwork"][0:16, g * CH:(g + 1) * CH],
                in_=wd["lgat"][rb:rb + 16, :]), dma=True)
        for j in range(4):
            ctx.ser("vector", lambda e, j=j: e.reduce_max(
                s["m4"][0:16, j:j + 1], s["lwork"][0:16, :], axis=AX.X))
            if j < 3:
                for sg in range(8):
                    ctx.ser("vector", lambda e, j=j, sg=sg: e.tensor_single_scalar(
                        s["sc"][0:16, 0:CH], s["lwork"][0:16, sg * CH:(sg + 1) * CH],
                        s["m4"][0:16, j:j + 1], ALU.is_ge))
                    ctx.ser("vector", lambda e, sg=sg: e.scalar_tensor_tensor(
                        s["lwork"][0:16, sg * CH:(sg + 1) * CH], s["sc"][0:16, 0:CH],
                        -1e30, s["lwork"][0:16, sg * CH:(sg + 1) * CH],
                        ALU.mult, ALU.add))
        for g in range(8):
            rb = own_chunk(g) * 32 + tslot_chunk(g) * 16
            ctx.ser("gpsimd", lambda e, g=g, rb=rb: e.dma_start(
                out=s["lwork"][0:16, g * CH:(g + 1) * CH],
                in_=wd["lgat"][rb:rb + 16, :]), dma=True)
        for sg in range(8):
            ctx.ser("vector", lambda e, sg=sg: e.tensor_single_scalar(
                s["sc"][0:16, 0:CH], s["lwork"][0:16, sg * CH:(sg + 1) * CH],
                s["m4"][0:16, 3:4], ALU.is_ge))
            ctx.ser("vector", lambda e, sg=sg: e.tensor_single_scalar(
                s["lwork"][0:16, sg * CH:(sg + 1) * CH],
                s["lwork"][0:16, sg * CH:(sg + 1) * CH], s["m4"][0:16, 0:1],
                ALU.subtract))
            ctx.ser("scalar", lambda e, sg=sg: e.activation(
                s["lwork"][0:16, sg * CH:(sg + 1) * CH],
                s["lwork"][0:16, sg * CH:(sg + 1) * CH], AF.Exp))
            ctx.ser("vector", lambda e, sg=sg: e.tensor_mul(
                s["lwork"][0:16, sg * CH:(sg + 1) * CH],
                s["lwork"][0:16, sg * CH:(sg + 1) * CH], s["sc"][0:16, 0:CH]))
        ctx.ser("vector", lambda e: e.reduce_sum(
            s["m4"][0:16, 4:5], s["lwork"][0:16, :], axis=AX.X))
        ctx.ser("vector", lambda e: e.reciprocal(
            s["m4"][0:16, 5:6], s["m4"][0:16, 4:5]))
        for sg in range(2):
            ctx.ser("vector", lambda e, sg=sg: e.tensor_single_scalar(
                s["sc"][0:16, 0:CH], s["lgstb"][0:16, sg * CH:(sg + 1) * CH],
                s["m4"][0:16, 3:4], ALU.is_ge))
            ctx.ser("vector", lambda e, sg=sg: e.tensor_single_scalar(
                s["lgstb"][0:16, sg * CH:(sg + 1) * CH],
                s["lgstb"][0:16, sg * CH:(sg + 1) * CH], s["m4"][0:16, 0:1],
                ALU.subtract))
            ctx.ser("scalar", lambda e, sg=sg: e.activation(
                s["lgstb"][0:16, sg * CH:(sg + 1) * CH],
                s["lgstb"][0:16, sg * CH:(sg + 1) * CH], AF.Exp))
            ctx.ser("vector", lambda e, sg=sg: e.tensor_mul(
                s["lgstb"][0:16, sg * CH:(sg + 1) * CH],
                s["lgstb"][0:16, sg * CH:(sg + 1) * CH], s["sc"][0:16, 0:CH]))
        if it == 0 and "eown" in DBG_TAPS:
            ctx.ser("sync", lambda e: e.dma_start(
                out=DBG_TAPS["eown"], in_=s["lgstb"][0:16, 0:TPC]), dma=True)
            ctx.ser("sync", lambda e: e.dma_start(
                out=DBG_TAPS["m4row"], in_=s["m4"][0:16, 0:8]), dma=True)
        for lci in range(8):
            ctx.ser("vector", lambda e, lci=lci: e.tensor_copy(
                s["tmpb"][0:16, 0:128], s["lgstb"][0:16, lci * 128:(lci + 1) * 128]))
            ctx.ser("tensor", lambda e: e.transpose(
                s["pbt"][0:128, 0:16], s["tmpb"][0:16, 0:128], s["idb"][0:16, 0:16]))
            ctx.ser("scalar", lambda e: e.activation(
                s["aT_sb"][:, 0:16], s["pbt"][0:128, 0:16], AF.Copy))
            for q4 in range(4):
                ctx.ser("tensor", lambda e, lci=lci, q4=q4: e.matmul(
                    s["pb"][q4][0:16, 0:CH], s["aT_sb"][:, 0:16],
                    s["flexA"][:, lci * 2048 + q4 * CH: lci * 2048 + (q4 + 1) * CH],
                    start=(lci == 0), stop=(lci == 7), skip_group_check=True))
        for q4 in range(4):
            ctx.ser("scalar", lambda e, q4=q4: e.activation(
                s["sc2"][0:16, 0:CH], s["pb"][q4][0:16, 0:CH], AF.Copy))
            ctx.ser("sync", lambda e, q4=q4: e.dma_start(
                out=wd["cc_wr"][0:16, q4 * CH:(q4 + 1) * CH], in_=s["sc2"][0:16, 0:CH]), dma=True)
        ctx.ser("gpsimd", lambda e: e.collective_compute(
            "AllReduce", ALU.add, [[0, 1, 2, 3], [4, 5, 6, 7]],
            [wd["cc_wr"].opt()], [wd["wr_red"].opt()]))
        for q4 in range(4):
            ctx.ser("sync", lambda e, q4=q4: e.dma_start(
                out=s["sc2"][0:16, 0:CH], in_=wd["wr_red"][0:16, q4 * CH:(q4 + 1) * CH]), dma=True)
            ctx.ser("vector", lambda e: e.tensor_scalar_mul(
                s["sc2"][0:16, 0:CH], s["sc2"][0:16, 0:CH], s["m4"][0:16, 5:6]))
            ctx.ser("vector", lambda e, q4=q4: e.tensor_add(
                s["ws_sb"][0:16, q4 * CH:(q4 + 1) * CH],
                s["ws_sb"][0:16, q4 * CH:(q4 + 1) * CH], s["sc2"][0:16, 0:CH]))
        ctx.constf(1e-6)
        ctx.ser("scalar", lambda e: e.activation(
            s["sc2"][0:16, 0:CH] if False else s["lgstb"][0:16, 0:TPC],
            s["ws_sb"][0:16, 0:TPC], AF.Square, accum_out=s["m4"][0:16, 6:7]))
        ctx.ser("scalar", lambda e: e.activation(
            s["lgstb"][0:16, 0:TPC], s["ws_sb"][0:16, TPC:D], AF.Square,
            accum_out=s["m4"][0:16, 7:8]))
        ctx.ser("vector", lambda e: e.tensor_add(
            s["m4"][0:16, 6:7], s["m4"][0:16, 6:7], s["m4"][0:16, 7:8]))
        ctx.ser("scalar", lambda e: e.activation(
            s["m4"][0:16, 7:8], s["m4"][0:16, 6:7], AF.Ln, scale=1.0 / D, bias=1e-6))
        ctx.ser("scalar", lambda e: e.activation(
            s["m4"][0:16, 7:8], s["m4"][0:16, 7:8], AF.Exp, scale=-0.5))
        ctx.ser("vector", lambda e: e.tensor_scalar_mul(
            s["ws_sb"][0:16, :], s["ws_sb"][0:16, :], s["m4"][0:16, 7:8]))
        for sl in range(4):
            ctx.ser("sync", lambda e, sl=sl: e.dma_start(
                out=s["row3"][0:1, 0:CH], in_=wd["npr_in"][0:1, sl * CH:(sl + 1) * CH]), dma=True)
            ctx.ser("tensor", lambda e: e.matmul(
                s["pb"][5][0:16, 0:CH], s["ones_row_f"][0:1, 0:16], s["row3"][0:1, 0:CH],
                start=True, stop=True))
            ctx.ser("scalar", lambda e: e.activation(
                s["sc2"][0:16, 0:CH], s["pb"][5][0:16, 0:CH], AF.Copy))
            ctx.ser("vector", lambda e, sl=sl: e.tensor_mul(
                s["ws_sb"][0:16, sl * CH:(sl + 1) * CH],
                s["ws_sb"][0:16, sl * CH:(sl + 1) * CH], s["sc2"][0:16, 0:CH]))
        if it == 0 and "ws0" in DBG_TAPS:
            ctx.ser("sync", lambda e: e.dma_start(
                out=DBG_TAPS["ws0"], in_=s["ws_sb"][0:16, :]), dma=True)
        for k in range(KD):
            ctx.ser("tensor", lambda e, k=k: e.transpose(
                s["pbt"][0:128, 0:16], s["ws_sb"][0:16, k * 128:(k + 1) * 128],
                s["idb"][0:16, 0:16]))
            ctx.ser("scalar", lambda e, k=k: e.activation(
                s["wsT_sb"][:, k * 16:(k + 1) * 16], s["pbt"][0:128, 0:16], AF.Copy))
        for sl in range(4):
            ctx.ser("gpsimd", lambda e, sl=sl: e.dma_start(
                out=s["wbuf"][:, 0:8192], in_=wd["inp"][4 + sl]), dma=True)
            for mci in range(4):
                mc = sl * 4 + mci
                for k in range(KD):
                    lhs = s["wbuf"][:, k * 512 + mci * 128: k * 512 + mci * 128 + 128]
                    ctx.ser("tensor", lambda e, l=lhs, k=k: e.matmul(
                        s["pb"][4][:, 0:16], l, s["wsT_sb"][:, k * 16:(k + 1) * 16],
                        start=(k == 0), stop=(k == KD - 1)))
                ctx.ser("scalar", lambda e, mc=mc: e.activation(
                    s["kkt"][:, mc * 16:(mc + 1) * 16], s["pb"][4][:, 0:16], AF.Copy,
                    scale=0.125))
        for sl in range(4):
            ctx.ser("gpsimd", lambda e, sl=sl: e.dma_start(
                out=s["wbuf"][:, 0:8192], in_=wd["inp"][8 + sl]), dma=True)
            for k in range(KD):
                lhs = s["wsT_sb"][:, k * 16:(k + 1) * 16]
                rhs = s["wbuf"][:, k * 512:(k + 1) * 512]
                ctx.ser("tensor", lambda e, l=lhs, r=rhs, k=k: e.matmul(
                    s["pb"][4][0:16, 0:CH], l, r, start=(k == 0), stop=(k == KD - 1)))
            ctx.ser("scalar", lambda e, sl=sl: e.activation(
                s["vvs"][0:16, sl * CH:(sl + 1) * CH], s["pb"][4][0:16, 0:CH], AF.Copy))
        for tt in range(2):
            t0 = tt * CH
            if it == 0:
                srcx = lambda k, t0=t0: xf[:, k * TPC + t0: k * TPC + t0 + CH]
                emit_stats(ctx, s, srcx, CH, 1.0, 1e-5, pfx="r")
                emit_xhat(ctx, s, srcx, lambda k: s["xb"][:, k * CH:(k + 1) * CH], CH)
            else:
                for k in range(KD):
                    ctx.ser("sync", lambda e, k=k, tt=tt: e.dma_start(
                        out=s["cvt2"][:, 0:CH],
                        in_=wd["xcur_dram"][tt][k * 128:(k + 1) * 128, :]), dma=True)
                    ctx.ser("vector", lambda e, k=k: e.tensor_copy(
                        s["xb"][:, k * CH:(k + 1) * CH], s["cvt2"][:, 0:CH]))
            for sl in range(4):
                ctx.ser("gpsimd", lambda e, sl=sl: e.dma_start(
                    out=s["wbuf"][:, 0:8192], in_=wd["inp"][sl]), dma=True)
                for mci in range(4):
                    mc = sl * 4 + mci
                    bank = s["pb"][mci % 2]
                    for k in range(KD):
                        lhs = s["wbuf"][:, k * 512 + mci * 128: k * 512 + mci * 128 + 128]
                        ctx.ser("tensor", lambda e, o=bank, l=lhs, k=k: e.matmul(
                            o[:, 0:CH], l, s["xb"][:, k * CH:(k + 1) * CH],
                            start=(k == 0), stop=(k == KD - 1)))
                    ctx.ser("scalar", lambda e, o=bank: e.activation(
                        s["sc"][:, 0:CH], o[:, 0:CH], AF.Copy))
                    ctx.ser("vector", lambda e, mc=mc: e.tensor_scalar_add(
                        s["sc"][:, 0:CH], s["sc"][:, 0:CH], s["bq_sb"][:, mc:mc + 1]))
                    ctx.ser("vector", lambda e, mc=mc: e.tensor_copy(
                        s["wbuf"][:, 8192 + mc * CH: 8192 + (mc + 1) * CH], s["sc"][:, 0:CH]))
            if it == 0 and tt == 0 and "q0" in DBG_TAPS:
                ctx.ser("sync", lambda e: e.dma_start(
                    out=DBG_TAPS["q0"], in_=s["wbuf"][:, 8192:16384]), dma=True)
            for h in range(32):
                pb_ = 64 * (h % 2)
                co = (h // 2)
                ctx.ser("tensor", lambda e, pb_=pb_, co=co: e.matmul(
                    s["pb"][2][0:16, 0:CH],
                    s["kkt"][pb_:pb_ + 64, co * 16:(co + 1) * 16],
                    s["wbuf"][pb_:pb_ + 64, 8192 + co * CH: 8192 + (co + 1) * CH],
                    start=True, stop=True))
                ctx.ser("scalar", lambda e: e.activation(
                    s["tmpb"][0:16, 0:CH], s["pb"][2][0:16, 0:CH], AF.Exp))
                ctx.ser("tensor", lambda e: e.matmul(
                    s["pb"][3][0:1, 0:CH], s["ones_col"][0:16, 0:1],
                    s["tmpb"][0:16, 0:CH], start=True, stop=True))
                ctx.ser("vector", lambda e: e.reciprocal(
                    s["row3"][0:1, 0:CH], s["pb"][3][0:1, 0:CH]))
                ctx.ser("tensor", lambda e: e.matmul(
                    s["pb"][5][0:64, 0:CH], s["ones_row_f"][0:1, 0:64],
                    s["row3"][0:1, 0:CH], start=True, stop=True))
                ctx.ser("scalar", lambda e: e.activation(
                    s["sc"][0:64, 0:CH], s["pb"][5][0:64, 0:CH], AF.Copy))
                ctx.ser("tensor", lambda e, h=h: e.matmul(
                    s["pb"][6][0:64, 0:CH], s["vvs"][0:16, h * 64:(h + 1) * 64],
                    s["tmpb"][0:16, 0:CH], start=True, stop=True))
                ctx.ser("vector", lambda e: e.tensor_tensor(
                    s["sc"][0:64, 0:CH], s["pb"][6][0:64, 0:CH], s["sc"][0:64, 0:CH],
                    ALU.mult))
                ctx.ser("vector", lambda e, h=h: e.tensor_scalar_add(
                    s["sc"][0:64, 0:CH], s["sc"][0:64, 0:CH],
                    s["bv_sb"][64 * (h % 2):64 * (h % 2) + 64, h // 2:h // 2 + 1]))
                ctx.ser("vector", lambda e, h=h, pb_=pb_, co=co: e.tensor_copy(
                    s["xb2"][pb_:pb_ + 64, co * CH:(co + 1) * CH], s["sc"][0:64, 0:CH]))
            if it == 0 and tt == 0 and "ctx0" in DBG_TAPS:
                ctx.ser("sync", lambda e: e.dma_start(
                    out=DBG_TAPS["ctx0"], in_=s["xb2"][:, :]), dma=True)
            if it == 0:
                emit_bcast(ctx, s, s["r_rstd"][0:1, 0:CH], 4, CH)
                emit_bcast(ctx, s, s["r_negms"][0:1, 0:CH], 3, CH)
            for sl in range(4):
                ctx.ser("gpsimd", lambda e, sl=sl: e.dma_start(
                    out=s["wbuf"][:, 0:8192], in_=wd["ops"][sl]), dma=True)
                for mci in range(4):
                    mc = sl * 4 + mci
                    bank = s["pb"][mci % 2]
                    for k in range(KD):
                        lhs = s["wbuf"][:, k * 512 + mci * 128: k * 512 + mci * 128 + 128]
                        ctx.ser("tensor", lambda e, o=bank, l=lhs, k=k: e.matmul(
                            o[:, 0:CH], l, s["xb2"][:, k * CH:(k + 1) * CH],
                            start=(k == 0), stop=(k == KD - 1)))
                    ctx.ser("scalar", lambda e, o=bank: e.activation(
                        s["sc"][:, 0:CH], o[:, 0:CH], AF.Copy))
                    ctx.ser("vector", lambda e, mc=mc: e.tensor_scalar_add(
                        s["sc"][:, 0:CH], s["sc"][:, 0:CH], s["opb_sb"][:, mc:mc + 1]))
                    if it == 0:
                        ctx.ser("vector", lambda e, mc=mc, t0=t0: e.tensor_tensor(
                            s["sc2"][:, 0:CH],
                            xf[:, mc * TPC + t0: mc * TPC + t0 + CH],
                            s["pb"][4][:, 0:CH], ALU.mult))
                        ctx.ser("vector", lambda e: e.tensor_tensor(
                            s["sc2"][:, 0:CH], s["sc2"][:, 0:CH], s["pb"][3][:, 0:CH],
                            ALU.add))
                        ctx.ser("vector", lambda e: e.tensor_add(
                            s["sc"][:, 0:CH], s["sc"][:, 0:CH], s["sc2"][:, 0:CH]))
                    else:
                        ctx.ser("sync", lambda e, mc=mc, tt=tt: e.dma_start(
                            out=s["cvt2"][:, 0:CH],
                            in_=wd["xcur_dram"][tt][mc * 128:(mc + 1) * 128, :]), dma=True)
                        ctx.ser("vector", lambda e: e.tensor_add(
                            s["sc"][:, 0:CH], s["sc"][:, 0:CH], s["cvt2"][:, 0:CH]))
                    ctx.ser("sync", lambda e, mc=mc, tt=tt: e.dma_start(
                        out=wd["tnew_dram"][tt][mc * 128:(mc + 1) * 128, :],
                        in_=s["sc"][:, 0:CH]), dma=True)
                    ctx.ser("scalar", lambda e: e.activation(
                        s["sc2"][:, 0:CH], s["sc"][:, 0:CH], AF.Square))
                    ctx.ser("tensor", lambda e, mc=mc: e.matmul(
                        s["pb"][6][0:1, 0:CH], s["ones_col_f"], s["sc2"][:, 0:CH],
                        start=(mc == 0), stop=(mc == 15), skip_group_check=True))
            ctx.ser("scalar", lambda e: e.activation(
                s["row1"][0:1, 0:CH], s["pb"][6][0:1, 0:CH], AF.Ln,
                scale=1.0 / D, bias=1e-6))
            ctx.ser("scalar", lambda e: e.activation(
                s["row1"][0:1, 0:CH], s["row1"][0:1, 0:CH], AF.Exp, scale=-0.5))
            ctx.ser("tensor", lambda e: e.matmul(
                s["pb"][5][:, 0:CH], s["ones_row_f"], s["row1"][0:1, 0:CH],
                start=True, stop=True))
            for mc in range(KD):
                ctx.ser("sync", lambda e, mc=mc, tt=tt: e.dma_start(
                    out=s["cvt2"][:, 0:CH],
                    in_=wd["tnew_dram"][tt][mc * 128:(mc + 1) * 128, :]), dma=True)
                ctx.ser("vector", lambda e: e.tensor_tensor(
                    s["sc"][:, 0:CH], s["cvt2"][:, 0:CH], s["pb"][5][:, 0:CH], ALU.mult))
                ctx.ser("scalar", lambda e, mc=mc: e.activation(
                    s["sc"][:, 0:CH], s["sc"][:, 0:CH], AF.Copy,
                    scale=s["npf_sb"][:, mc:mc + 1]))
                if final:
                    xsl = xf[:, mc * TPC + t0: mc * TPC + t0 + CH]
                    ctx.ser("vector", lambda e, x=xsl: e.tensor_add(
                        x, x, s["sc"][:, 0:CH]))
                else:
                    ctx.ser("sync", lambda e, mc=mc, tt=tt: e.dma_start(
                        out=wd["xcur_dram"][tt][mc * 128:(mc + 1) * 128, :],
                        in_=s["sc"][:, 0:CH]), dma=True)
                    if tt == 0 and "xc0" in DBG_TAPS:
                        ctx.ser("sync", lambda e, mc=mc: e.dma_start(
                            out=DBG_TAPS["xc0"][mc * 128:(mc + 1) * 128, :],
                            in_=s["sc"][:, 0:CH]), dma=True)



RSC = 0.08838834764831843


def emit_l1_attn_b(ctx, s, xf, wd):
    nc = ctx.nc
    ctx.constf(-4.0)
    for tt in range(2):
        t0 = tt * CH
        src = lambda k, t0=t0: xf[:, k * TPC + t0: k * TPC + t0 + CH]
        emit_stats(ctx, s, src, CH, 1.0, 1e-5, pfx="r")
        emit_xhat(ctx, s, src, lambda k: s["xb"][:, k * CH:(k + 1) * CH], CH)
        for slab in range(12):
            ctx.ser("gpsimd", lambda e, slab=slab: e.dma_start(
                out=s["wbuf"][:, 0:8192], in_=wd["was"][slab]), dma=True)
            if slab < 8:
                for mci in range(4):
                    mc = (slab % 4) * 4 + mci
                    for k in range(KD):
                        lhs = s["wbuf"][:, k * 512 + mci * 128: k * 512 + mci * 128 + 128]
                        ctx.ser("tensor", lambda e, l=lhs, k=k: e.matmul(
                            s["pb"][0][:, 0:CH], l, s["xb"][:, k * CH:(k + 1) * CH],
                            start=(k == 0), stop=(k == KD - 1)))
                    if slab < 4:
                        ctx.ser("scalar", lambda e, mc=mc, tt=tt: e.activation(
                            s["flexA"][:, mc * TPC + tt * CH: mc * TPC + tt * CH + CH],
                            s["pb"][0][:, 0:CH], AF.Copy))
                    else:
                        ctx.ser("scalar", lambda e: e.activation(
                            s["tmpb"][:, 0:CH], s["pb"][0][:, 0:CH], AF.Copy))
                        ctx.ser("sync", lambda e, mc=mc, tt=tt: e.dma_start(
                            out=wd["kf_sp"][mc * 128:(mc + 1) * 128, tt * CH:(tt + 1) * CH],
                            in_=s["tmpb"][:, 0:CH]), dma=True)
            else:
                for tsc in range(4):
                    for k in range(KD):
                        lhs = s["xb"][:, k * CH + tsc * 128: k * CH + tsc * 128 + 128]
                        rhs = s["wbuf"][:, k * 512:(k + 1) * 512]
                        ctx.ser("tensor", lambda e, l=lhs, r=rhs, k=k: e.matmul(
                            s["pb"][0][:, 0:CH], l, r, start=(k == 0), stop=(k == KD - 1)))
                    ctx.ser("scalar", lambda e: e.activation(
                        s["tmpb"][:, 0:CH], s["pb"][0][:, 0:CH], AF.Copy))
                    ctx.ser("sync", lambda e, slab=slab, tsc=tsc, tt=tt: e.dma_start(
                        out=wd["vt_sp"][tt * CH + tsc * 128: tt * CH + tsc * 128 + 128,
                                        (slab - 8) * CH:(slab - 7) * CH],
                        in_=s["tmpb"][:, 0:CH]), dma=True)
    ctx.ser("gpsimd", lambda e: e.collective_compute(
        "AllGather", ALU.bypass, [[0, 1, 2, 3], [4, 5, 6, 7]],
        [wd["kf_sp"].opt()], [wd["kgat"].opt()]))
    ctx.ser("gpsimd", lambda e: e.collective_compute(
        "AllGather", ALU.bypass, [[0, 1, 2, 3], [4, 5, 6, 7]],
        [wd["vt_sp"].opt()], [wd["vgat"].opt()]))
    for tq in range(2):
        t0 = tq * CH
        U = 4 if tq == 0 else 8
        ub = 0 if tq == 0 else 4
        for u in range(U):
            ctx.ser("sync", lambda e, u=u, ub=ub: e.dma_start(
                out=s["cmaskA"][:, u * 2048:(u + 1) * 2048],
                in_=wd["cm_in"][ub + u]), dma=True)
        for h in range(16):
            for u in range(U):
                rj, tj = (u, 0) if u < 4 else (7 - u, 1)
                for kk in range(4):
                    if kk == 0:
                        ctx.ser("sync", lambda e, rj=rj, tj=tj, h=h: e.dma_start(
                            out=s["khqh"][:, 0:CH],
                            in_=wd["kgat"][rj * 2048 + h * 128: rj * 2048 + (h + 1) * 128,
                                           tj * CH:(tj + 1) * CH]), dma=True)
                    first = (u == 0 and kk == 0)
                    last = (u == U - 1 and kk == 3)
                    ctx.ser("tensor", lambda e, kk=kk, h=h, tq=tq: e.matmul(
                        s["pb"][0][:, 0:CH],
                        s["khqh"][:, kk * 128:(kk + 1) * 128],
                        s["flexA"][:, h * TPC + tq * CH: h * TPC + tq * CH + CH],
                        start=True, stop=True))
                    ctx.ser("vector", lambda e, u=u, kk=kk: e.tensor_tensor(
                        s["sc"][:, 0:CH], s["pb"][0][:, 0:CH],
                        s["cmaskA"][:, u * 2048 + kk * CH: u * 2048 + (kk + 1) * CH],
                        ALU.add))
                    ctx.ser("scalar", lambda e: e.activation(
                        s["tmpb"][:, 0:CH], s["sc"][:, 0:CH], AF.Exp,
                        scale=RSC, bias=-4.0))
                    ctx.ser("tensor", lambda e, first=first, last=last: e.matmul(
                        s["pb"][2][0:1, 0:CH], s["ones_col"], s["tmpb"][:, 0:CH],
                        start=first, stop=last, skip_group_check=True))
                    ctx.ser("sync", lambda e, rj=rj, tj=tj, kk=kk, h=h: e.dma_start(
                        out=s["khqh"][:, CH:CH + 128],
                        in_=wd["vgat"][rj * 1024 + tj * CH + kk * 128:
                                       rj * 1024 + tj * CH + (kk + 1) * 128,
                                       h * 128:(h + 1) * 128]), dma=True)
                    ctx.ser("tensor", lambda e, first=first, last=last: e.matmul(
                        s["pb"][1][:, 0:CH], s["khqh"][:, CH:CH + 128],
                        s["tmpb"][:, 0:CH], start=first, stop=last,
                        skip_group_check=True))
            ctx.ser("vector", lambda e: e.reciprocal(
                s["row3"][0:1, 0:CH], s["pb"][2][0:1, 0:CH]))
            ctx.ser("tensor", lambda e: e.matmul(
                s["pb"][5][:, 0:CH], s["ones_row_f"], s["row3"][0:1, 0:CH],
                start=True, stop=True))
            ctx.ser("scalar", lambda e: e.activation(
                s["sc"][:, 0:CH], s["pb"][5][:, 0:CH], AF.Copy))
            ctx.ser("vector", lambda e, h=h: e.tensor_tensor(
                s["xb"][:, h * CH:(h + 1) * CH], s["pb"][1][:, 0:CH],
                s["sc"][:, 0:CH], ALU.mult))
        for slab in range(4):
            ctx.ser("gpsimd", lambda e, slab=slab: e.dma_start(
                out=s["wbuf"][:, 0:8192], in_=wd["wps"][slab]), dma=True)
            for mci in range(4):
                mc = slab * 4 + mci
                bank = s["pb"][3 + (mci % 2)]
                for k in range(KD):
                    lhs = s["wbuf"][:, k * 512 + mci * 128: k * 512 + mci * 128 + 128]
                    ctx.ser("tensor", lambda e, o=bank, l=lhs, k=k: e.matmul(
                        o[:, 0:CH], l, s["xb"][:, k * CH:(k + 1) * CH],
                        start=(k == 0), stop=(k == KD - 1)))
                xsl = xf[:, mc * TPC + t0: mc * TPC + t0 + CH]
                ctx.ser("vector", lambda e, o=bank, x=xsl: e.tensor_tensor(
                    x, x, o[:, 0:CH], ALU.add))



def alloc_state(ctx, stack, blocks=()):
    nc = ctx.nc
    s = {}
    A = lambda name, shape, dt: stack.enter_context(nc.sbuf_tensor(name, shape, dt))
    s["pb"] = [stack.enter_context(nc.psum_tensor(f"pb{i}", [128, 512], F32))
               for i in range(7)]
    s["pbt"] = stack.enter_context(nc.psum_tensor("pbt", [128, 512], BF16))
    need_a = "l0_attn_a" in blocks
    need_ffn = any(b.startswith("l0_ffn") for b in blocks)
    xbw = 640 if need_a else CH
    s["xb"] = A("xb", [128, KD * xbw], BF16)
    if need_a or need_ffn:
        s["xb2"] = A("xb2", [128, KD * CH], BF16)
    s["wbuf"] = A("wbuf", [128, 2 * 8192], BF16)
    s["flexA"] = A("flexA", [128, 32 * CH], BF16)
    s["sc"] = A("sc", [128, CH], F32)
    s["sc2"] = A("sc2", [128, CH], F32)
    s["tmpb"] = A("tmpb", [128, 640], BF16)
    if need_a or "l0_attn_b" in blocks or "l1_attn_b" in blocks:
        s["khqh"] = A("khqh", [128, 640 + CH], BF16)
    if "l1_attn_b" in blocks:
        s["cmaskA"] = A("cmaskA", [128, 8 * 2048], BF16)
    if need_a:
        s["flexF"] = A("flexF", [128, 2048], F32)
        s["wmask"] = A("wmask", [128, 5 * CH], BF16)
        s["kill_sb"] = A("kill_sb", [128, 2], F32)
        s["g_sb"] = A("g_sb", [128, KD], F32)
        s["rowb"] = A("rowb", [1, 640], BF16)
        for wn in ("wq", "wk", "wv", "wo"):
            s[wn + "_sh"] = A(wn + "_sh", [128, 1], F32)
            s[wn + "_nh"] = A(wn + "_nh", [128, 1], F32)
        s["wq_se"] = A("wq_se", [128, 1], F32)
    roww = 640 if need_a else CH
    for nm in ("r_rstd", "r_negms", "q_rstd", "q_negms", "row1", "row2", "row3"):
        s[nm] = A(nm, [1, roww], F32)
    s["ones_col"] = A("ones_col", [128, 1], BF16)[:]
    s["ones_col_f"] = A("ones_col_f", [128, 1], F32)[:]
    s["ones_row_f"] = A("ones_row_f", [1, 128], F32)[:]
    s["ones_row_b"] = A("ones_row_b", [1, 128], BF16)[:]
    s["H128"] = A("H128", [128, 128], BF16)[:]
    s["T2"] = A("T2", [128, 128], BF16)[:]
    s["idb"] = A("idb_t", [128, 128], BF16)[:]
    s["idf"] = A("idf_t", [128, 128], F32)[:]
    s["cid_sb"] = A("cid_t", [1, 1], I32)
    if "l0_attn_b" in blocks:
        s["cvt"] = A("cvt", [128, 520], F32)
        s["dts"] = A("dts", [64, 512], F32)
        s["ssb"] = A("ssb", [64, 513], F32)
        s["srow"] = A("srow", [1, 513], F32)
        s["srowneg"] = A("srowneg", [1, 513], F32)
        s["gts"] = A("gts", [128, 512], F32)
        s["btm"] = A("btm", [128, 512], BF16)
        s["dttm"] = A("dttm", [128, 256], F32)
        s["state_sb"] = A("state_sb", [128, 4096], BF16)
        s["decqb"] = A("decqb", [128, 256], F32)
        s["rql"] = A("rql", [128, 256], F32)
        s["decb"] = A("decb", [128, 64], F32)
        s["rq64"] = A("rq64", [1, 128], F32)
        s["hstage"] = A("hstage", [128, 102], BF16)
        s["bndst"] = A("bndst", [128, 204], BF16)
        s["tri_sb"] = A("tri_sb", [128, 128], F32)
        s["mup_sb"] = A("mup_sb", [128, 67], F32)
        s["convw_sb"] = A("convw_sb", [128, 136], F32)
        s["convb_sb"] = A("convb_sb", [128, 34], F32)
        s["Dbc_sb"] = A("Dbc_sb", [128, 64], F32)
        s["nw_sb"] = A("nw_sb", [128, 32], F32)
        s["ea_sb"] = A("ea_sb", [64, 1], F32)
        s["dtb_sb"] = A("dtb_sb", [64, 1], F32)
        s["bidx_sb"] = A("bidx_sb", [128, 2], I32)
    if "l1_attn_a" in blocks:
        if "xb2" not in s:
            s["xb2"] = A("xb2", [128, KD * CH], BF16)
        for nm, shp, dt in (("etd", [128, 16], F32), ("wcol", [128, 16], F32),
                            ("efa", [128, 16], F32), ("lw", [128, 16], F32),
                            ("onesT2", [128, CH], F32), ("st16", [128, 32], F32),
                            ("hh", [128, 16], F32), ("Lg", [128, 128], F32),
                            ("carry", [128, 16], F32), ("wj", [128, 16], F32),
                            ("xml", [128, 16], F32), ("colA", [128, 1], F32),
                            ("colB", [128, 1], F32), ("l1av_sb", [128, 160], F32),
                            ("mfac_sb", [128, 16], F32), ("fam_sb", [128, 2], F32),
                            ("cvt2", [128, CH], F32)):
            s[nm] = A(nm, shp, dt)
        s["bidx2_sb"] = A("bidx2_sb", [128, 2], I32)
    if "l1_ffn" in blocks:
        if "xb2" not in s:
            s["xb2"] = A("xb2", [128, KD * CH], BF16)
        if "cvt2" not in s:
            s["cvt2"] = A("cvt2", [128, CH], F32)
        s["ws_sb"] = A("ws_sb", [16, D], BF16)
        s["lgstb"] = A("lgstb", [16, TPC], BF16)
        s["lwork"] = A("lwork", [16, L], BF16)
        s["m4"] = A("m4", [16, 8], F32)
        s["aT_sb"] = A("aT_sb", [128, 16], BF16)
        s["wsT_sb"] = A("wsT_sb", [128, 256], BF16)
        s["kkt"] = A("kkt", [128, 256], BF16)
        s["vvs"] = A("vvs", [16, D], BF16)
        s["cb_sb"] = A("cb_sb", [16, 1], F32)
        s["bq_sb"] = A("bq_sb", [128, 16], F32)
        s["bv_sb"] = A("bv_sb", [128, 16], F32)
        s["opb_sb"] = A("opb_sb", [128, 16], F32)
        s["npf_sb"] = A("npf_sb", [128, 16], F32)
    return s


def emit_init_consts(ctx, s, cst):
    ctx.ser("vector", lambda e: e.memset(s["ones_col"][:], 1.0))
    ctx.ser("vector", lambda e: e.memset(s["ones_col_f"][:], 1.0))
    ctx.ser("vector", lambda e: e.memset(s["ones_row_f"][:], 1.0))
    ctx.ser("vector", lambda e: e.memset(s["ones_row_b"][:], 1.0))
    ctx.ser("sync", lambda e: e.dma_start(out=s["H128"][:], in_=cst["h128"]), dma=True)
    ctx.ser("sync", lambda e: e.dma_start(out=s["T2"][:], in_=cst["t2"]), dma=True)
    ctx.ser("sync", lambda e: e.dma_start(out=s["idb"][:], in_=cst["idb"]), dma=True)
    ctx.ser("sync", lambda e: e.dma_start(out=s["idf"][:], in_=cst["idf"]), dma=True)
    ctx.ser("sync", lambda e: e.dma_start(out=s["cid_sb"][:], in_=cst["cid"]), dma=True)
    ctx.cid_init(s["cid_sb"])


def build_program(blocks):
    import contextlib
    nc = bass.Bass("TRN2", target_bir_lowering=False)
    x_in = nc.dram_tensor("x_in", [D, TPC], F32, kind="ExternalInput").ap()
    x_out = nc.dram_tensor("x_out", [D, TPC], F32, kind="ExternalOutput").ap()
    cst = {
        "h128": nc.dram_tensor("h128", [128, 128], BF16, kind="ExternalInput").ap(),
        "t2": nc.dram_tensor("t2", [128, 128], BF16, kind="ExternalInput").ap(),
        "idb": nc.dram_tensor("idb", [128, 128], BF16, kind="ExternalInput").ap(),
        "idf": nc.dram_tensor("idf", [128, 128], F32, kind="ExternalInput").ap(),
        "cid": nc.dram_tensor("cid", [1, 1], I32, kind="ExternalInput").ap(),
    }
    wd = {}
    if any(b.startswith("l0_ffn") for b in blocks):
        wd["w1"] = nc.dram_tensor("w1s", [16, 128, 8192], F32, kind="ExternalInput").ap()
        wd["w3"] = nc.dram_tensor("w3s", [16, 128, 8192], F32, kind="ExternalInput").ap()
        wd["w2s"] = nc.dram_tensor("w2s", [8192, 2048], F32, kind="ExternalInput").ap()
        wd["v2h"] = nc.dram_tensor("v2h", [8192, 2048], BF16, kind="Internal").ap()
    if "l0_attn_a" in blocks:
        for wn in ("wq", "wk", "wv", "wo"):
            wd[wn + "s"] = nc.dram_tensor(wn + "s", [8, 128, 4096], F32,
                                          kind="ExternalInput").ap()
            wd["q" + wn] = nc.dram_tensor("q" + wn, [8, 128, 4096], BF16,
                                          kind="Internal").ap()
        wd["x_halo"] = nc.dram_tensor("x_halo", [D, 256], F32, kind="ExternalInput").ap()
        wd["g_in"] = nc.dram_tensor("g_in", [128, KD], F32, kind="ExternalInput").ap()
        wd["kill_in"] = nc.dram_tensor("kill_in", [128, 2], F32, kind="ExternalInput").ap()
        wd["wmask_in"] = nc.dram_tensor("wmask_in", [128, 5 * CH], BF16,
                                        kind="ExternalInput").ap()
    if "l0_attn_b" in blocks:
        wd["wins"] = nc.dram_tensor("wins", [22, 128, 6144], F32, kind="ExternalInput").ap()
        wd["windt"] = nc.dram_tensor("windt", [128, 1024], F32, kind="ExternalInput").ap()
        wd["wouts"] = nc.dram_tensor("wouts", [8, 128, 8192], F32, kind="ExternalInput").ap()
        wd["z_dram"] = nc.dram_tensor("z_dram", [4096, TPC], BF16, kind="Internal").ap()
        wd["xbc_dram"] = nc.dram_tensor("xbc_dram", [2, 4352, 512], BF16, kind="Internal").ap()
        wd["ytm_dram"] = nc.dram_tensor("ytm_dram", [2, 4, 128, 4096], BF16, kind="Internal").ap()
        wd["L_dram"] = nc.dram_tensor("L_dram", [2, 128, 4096], BF16, kind="Internal").ap()
        wd["stin_dram"] = nc.dram_tensor("stin_dram", [2, 128, 4096], BF16, kind="Internal").ap()
        wd["dts_dram"] = nc.dram_tensor("dts_dram", [2, 64, 512], F32, kind="Internal").ap()
        wd["ssb_dram"] = nc.dram_tensor("ssb_dram", [2, 64, 513], F32, kind="Internal").ap()
        wd["sbase_dram"] = nc.dram_tensor("sbase_dram", [10, 64], F32, kind="Internal").ap()
        wd["bc_dram"] = nc.dram_tensor("bc_dram", [2, 128, 1024], BF16, kind="Internal").ap()
        wd["cc_bnd"] = nc.dram_tensor("cc_bnd", [256, 102], BF16, kind="Internal").ap()
        wd["bnd_gat"] = nc.dram_tensor("bnd_gat", [1152, 102], BF16, kind="Internal").ap()
        wd["cc_st"] = nc.dram_tensor("cc_st", [128, 4096], BF16, kind="Internal").ap()
        wd["st_gat"] = nc.dram_tensor("st_gat", [512, 4096], BF16, kind="Internal").ap()
        wd["tri_in"] = nc.dram_tensor("tri_in", [128, 128], F32, kind="ExternalInput").ap()
        wd["mup_in"] = nc.dram_tensor("mup_in", [128, 67], F32, kind="ExternalInput").ap()
        wd["convw_in"] = nc.dram_tensor("convw_in", [128, 136], F32, kind="ExternalInput").ap()
        wd["convb_in"] = nc.dram_tensor("convb_in", [128, 34], F32, kind="ExternalInput").ap()
        wd["Dbc_in"] = nc.dram_tensor("Dbc_in", [128, 64], F32, kind="ExternalInput").ap()
        wd["nw_in"] = nc.dram_tensor("nw_in", [128, 32], F32, kind="ExternalInput").ap()
        wd["alog_in"] = nc.dram_tensor("alog_in", [64, 1], F32, kind="ExternalInput").ap()
        wd["dtb_in"] = nc.dram_tensor("dtb_in", [64, 1], F32, kind="ExternalInput").ap()
        wd["bidx_in"] = nc.dram_tensor("bidx_in", [128, 2], I32, kind="ExternalInput").ap()
    if "l1_attn_a" in blocks:
        for nm, shp in (("wr", [4, 128, 8192]), ("wk", [4, 128, 8192]),
                        ("wv", [4, 128, 8192]), ("wo2", [4, 128, 8192]),
                        ("wcr", [4, 128, 8192]), ("wck", [12, 128, 8192]),
                        ("wcv", [16, 128, 6144])):
            wd[nm] = nc.dram_tensor(nm, shp, F32, kind="ExternalInput").ap()
        wd["kv_dram"] = nc.dram_tensor("kv_dram", [2, 2048, 512], BF16, kind="Internal").ap()
        wd["xmid_dram"] = nc.dram_tensor("xmid_dram", [2, 2048, 512], F32, kind="Internal").ap()
        wd["cc16"] = nc.dram_tensor("cc16", [256, 16], F32, kind="Internal").ap()
        wd["gat16"] = nc.dram_tensor("gat16", [1152, 16], F32, kind="Internal").ap()
        wd["l1av_in"] = nc.dram_tensor("l1av_in", [128, 160], F32, kind="ExternalInput").ap()
        wd["mfac_in"] = nc.dram_tensor("mfac_in", [128, 16], F32, kind="ExternalInput").ap()
        wd["fam_in"] = nc.dram_tensor("fam_in", [128, 2], F32, kind="ExternalInput").ap()
        wd["bidx2_in"] = nc.dram_tensor("bidx2_in", [128, 2], I32, kind="ExternalInput").ap()
    if "l1_attn_b" in blocks:
        wd["was"] = nc.dram_tensor("was", [12, 128, 8192], F32, kind="ExternalInput").ap()
        wd["wps"] = nc.dram_tensor("wps", [4, 128, 8192], F32, kind="ExternalInput").ap()
        wd["cm_in"] = nc.dram_tensor("cm_in", [12, 128, 2048], BF16,
                                     kind="ExternalInput").ap()
        wd["kf_sp"] = nc.dram_tensor("kf_sp", [2048, 1024], BF16, kind="Internal").ap()
        wd["kgat"] = nc.dram_tensor("kgat", [8192, 1024], BF16, kind="Internal").ap()
        wd["vt_sp"] = nc.dram_tensor("vt_sp", [1024, 2048], BF16, kind="Internal").ap()
        wd["vgat"] = nc.dram_tensor("vgat", [4096, 2048], BF16, kind="Internal").ap()
    if "l1_ffn" in blocks:
        wd["cws"] = nc.dram_tensor("cws", [1, 128, 256], F32, kind="ExternalInput").ap()
        wd["wws"] = nc.dram_tensor("wws", [4, 128, 8192], F32, kind="ExternalInput").ap()
        wd["inp"] = nc.dram_tensor("inp", [12, 128, 8192], F32, kind="ExternalInput").ap()
        wd["ops"] = nc.dram_tensor("ops", [4, 128, 8192], F32, kind="ExternalInput").ap()
        wd["wb_in"] = nc.dram_tensor("wb_in", [1, D], F32, kind="ExternalInput").ap()
        wd["npr_in"] = nc.dram_tensor("npr_in", [1, D], F32, kind="ExternalInput").ap()
        wd["wksp_in"] = nc.dram_tensor("wksp_in", [16, D], F32, kind="ExternalInput").ap()
        wd["lffnv_in"] = nc.dram_tensor("lffnv_in", [4, 128, 16], F32, kind="ExternalInput").ap()
        wd["cbv_in"] = nc.dram_tensor("cbv_in", [16, 1], F32, kind="ExternalInput").ap()
        wd["xcur_dram"] = nc.dram_tensor("xcur_dram", [2, D, CH], F32, kind="Internal").ap()
        wd["tnew_dram"] = nc.dram_tensor("tnew_dram", [2, D, CH], F32, kind="Internal").ap()
        wd["cc_lg"] = nc.dram_tensor("cc_lg", [32, CH], F32, kind="Internal").ap()
        wd["lgat"] = nc.dram_tensor("lgat", [128, CH], F32, kind="Internal").ap()
        wd["cc_wr"] = nc.dram_tensor("cc_wr", [16, D], F32, kind="Internal").ap()
        wd["wr_red"] = nc.dram_tensor("wr_red", [16, D], F32, kind="Internal").ap()
    for nm, (shp, dt) in DBG_TAPS_REQ.items():
        DBG_TAPS[nm] = nc.dram_tensor("dbg_" + nm, shp, dt, kind="ExternalOutput").ap()

    with contextlib.ExitStack() as stack:
        ctx = Ctx(nc, stack)
        s = alloc_state(ctx, stack, blocks)
        xf = stack.enter_context(nc.sbuf_tensor("xf", [128, KD * TPC], F32))
        block = stack.enter_context(nc.Block())
        ctx.sem("CH")
        emit_init_consts(ctx, s, cst)
        if "l0_attn_a" in blocks:
            ctx.ser("sync", lambda e: e.dma_start(out=s["g_sb"][:], in_=wd["g_in"]), dma=True)
            ctx.ser("sync", lambda e: e.dma_start(out=s["kill_sb"][:], in_=wd["kill_in"]), dma=True)
            ctx.ser("sync", lambda e: e.dma_start(out=s["wmask"][:], in_=wd["wmask_in"]), dma=True)
        if "l0_attn_b" in blocks:
            for nm, dst in (("tri_in", "tri_sb"), ("mup_in", "mup_sb"),
                            ("convw_in", "convw_sb"), ("convb_in", "convb_sb"),
                            ("Dbc_in", "Dbc_sb"), ("nw_in", "nw_sb"),
                            ("alog_in", "ea_sb"), ("dtb_in", "dtb_sb"),
                            ("bidx_in", "bidx_sb")):
                ctx.ser("sync", lambda e, nm=nm, dst=dst: e.dma_start(
                    out=s[dst][:], in_=wd[nm]), dma=True)
            ctx.ser("scalar", lambda e: e.activation(
                s["ea_sb"][0:64, 0:1], s["ea_sb"][0:64, 0:1], AF.Exp))
            ctx.ser("vector", lambda e: e.memset(s["gts"][:], 1.0))
        if "l1_attn_a" in blocks:
            for nm, dst in (("l1av_in", "l1av_sb"), ("mfac_in", "mfac_sb"),
                            ("fam_in", "fam_sb"), ("bidx2_in", "bidx2_sb")):
                ctx.ser("sync", lambda e, nm=nm, dst=dst: e.dma_start(
                    out=s[dst][:], in_=wd[nm]), dma=True)
        if "l1_ffn" in blocks:
            ctx.ser("sync", lambda e: e.dma_start(
                out=s["cb_sb"][:], in_=wd["cbv_in"]), dma=True)
            for i, dst in enumerate(("bq_sb", "bv_sb", "opb_sb", "npf_sb")):
                ctx.ser("sync", lambda e, dst=dst, i=i: e.dma_start(
                    out=s[dst][:], in_=wd["lffnv_in"][i]), dma=True)
        emit_load_x(ctx, xf, x_in)
        made_v2h = False
        for b in blocks:
            if b.startswith("l0_ffn"):
                if not made_v2h:
                    emit_l0_ffn_v2h(ctx, s, wd)
                    made_v2h = True
                emit_l0_ffn(ctx, s, xf, wd)
            elif b == "l0_attn_a":
                emit_l0_attn_a(ctx, s, xf, wd)
            elif b == "l0_attn_b":
                emit_l0_attn_b(ctx, s, xf, wd, None)
            elif b == "l1_attn_a":
                emit_l1_attn_a(ctx, s, xf, wd)
            elif b == "l1_ffn":
                emit_l1_ffn(ctx, s, xf, wd)
            elif b == "l1_attn_b":
                emit_l1_attn_b(ctx, s, xf, wd)
        emit_store_x(ctx, xf, x_out)
        ctx.finish(block)
    return nc



def shard_x(x_full):
    out = []
    for c in range(NCORES):
        sq, ca, cb = core_chunks(c)
        xc = np.concatenate([x_full[sq, ca * CH:(ca + 1) * CH, :],
                             x_full[sq, cb * CH:(cb + 1) * CH, :]], axis=0)
        out.append(np.ascontiguousarray(xc.T.astype(np.float32)))
    return out


def unshard_x(per_core):
    x = np.zeros((B, L, D), np.float32)
    for c in range(NCORES):
        sq, ca, cb = core_chunks(c)
        xc = np.asarray(per_core[c], np.float32).T
        x[sq, ca * CH:(ca + 1) * CH, :] = xc[:CH]
        x[sq, cb * CH:(cb + 1) * CH, :] = xc[CH:]
    return x


def const_inputs():
    return {
        "h128": hadamard(128).astype(BF),
        "t2": np.kron(np.eye(2, dtype=np.float32), hadamard(64)).astype(BF),
        "idb": np.eye(128, dtype=np.float32).astype(BF),
        "idf": np.eye(128, dtype=np.float32),
    }


def stage_l0_attn_b(W_in, conv_w, conv_b, A_log, Dv, dt_bias, norm_w, W_out, mup):
    W_in = np.asarray(W_in, np.float32)
    p_ = np.arange(128)[:, None]
    f_ = np.arange(128)[None, :]
    mup = np.asarray(mup, np.float32)
    return {
        "wins": stage_lhsT(np.ascontiguousarray(W_in[:, 0:8448]), 384),
        "windt": np.ascontiguousarray(
            W_in[:, 8448:8512].reshape(16, 128, 64).transpose(1, 0, 2).reshape(128, 1024)),
        "wouts": stage_lhsT(np.asarray(W_out, np.float32), 256),
        "tri_in": np.where(p_ <= f_, 0.0, -1e30).astype(np.float32),
        "mup_in": np.ascontiguousarray(
            np.concatenate([mup, np.zeros(64, np.float32)]).reshape(67, 128).T),
        "convw_in": np.ascontiguousarray(
            np.asarray(conv_w, np.float32).reshape(34, 128, 4).transpose(1, 0, 2).reshape(128, 136)),
        "convb_in": np.ascontiguousarray(np.asarray(conv_b, np.float32).reshape(34, 128).T),
        "Dbc_in": np.ascontiguousarray(
            np.broadcast_to(np.asarray(Dv, np.float32), (128, 64))),
        "nw_in": np.ascontiguousarray(np.asarray(norm_w, np.float32).reshape(32, 128).T),
        "alog_in": np.asarray(A_log, np.float32).reshape(64, 1),
        "dtb_in": np.asarray(dt_bias, np.float32).reshape(64, 1),
    }


def stage_l1_attn_a(d):
    g = lambda k: np.asarray(d[k], np.float32)
    vecs = np.zeros((128, 160), np.float32)
    for i, k in enumerate(("norm1_w", "norm1_b", "norm2_w", "norm2_b", "time_decay",
                           "time_faaaa", "time_mix_r", "time_mix_k", "time_mix_v",
                           "cm_mix")):
        vecs[:, i * 16:(i + 1) * 16] = g("L1_attn_a_" + k).reshape(16, 128).T
    return {
        "wr": stage_lhsT(g("L1_attn_a_Wr"), 512),
        "wk": stage_lhsT(g("L1_attn_a_Wk"), 512),
        "wv": stage_lhsT(g("L1_attn_a_Wv"), 512),
        "wo2": stage_lhsT(g("L1_attn_a_Wo"), 512),
        "wcr": stage_lhsT(g("L1_attn_a_Wcr"), 512),
        "wck": stage_lhsT(g("L1_attn_a_Wck"), 512),
        "wcv": stage_lhsT(g("L1_attn_a_Wcv"), 128),
        "l1av_in": vecs,
    }


def stage_l1_attn_b(d):
    g = lambda k: np.asarray(d["L1_attn_b_" + k], np.float32)
    return {
        "was": stage_lhsT(np.ascontiguousarray(g("W_attn")), 512),
        "wps": stage_lhsT(np.ascontiguousarray(g("W_proj")), 512),
    }


def cattn_percore(c):
    import ml_dtypes
    v = c % 4
    cm = np.zeros((12, 128, 2048), np.float32)
    r = np.arange(128)[:, None]
    q = np.arange(CH)[None, :]
    diag = np.zeros((128, 2048), np.float32)
    for kk in range(4):
        diag[:, kk * CH:(kk + 1) * CH] = np.where(q >= kk * 128 + r, 0.0, -1e30)
    for tq, qc, ub, U in ((0, v, 0, 4), (1, 7 - v, 4, 8)):
        for u in range(U):
            if u == qc:
                cm[ub + u] = diag
            elif u > qc:
                cm[ub + u] = -1e30
    return {"cm_in": cm.astype(ml_dtypes.bfloat16)}


def stage_l1_ffn(d):
    g = lambda k: np.asarray(d["L1_ffn_" + k], np.float32)
    ipb = g("in_proj_b")
    vecs = np.zeros((4, 128, 16), np.float32)
    vecs[0] = ipb[:2048].reshape(16, 128).T
    vecs[1] = ipb[4096:6144].reshape(16, 128).T
    vecs[2] = g("out_proj_b").reshape(16, 128).T
    vecs[3] = g("norm_post_w").reshape(16, 128).T
    return {
        "cws": stage_lhsT(np.ascontiguousarray(
            (g("compete_w") * g("norm_pre_w")[None, :]).T), 16),
        "wws": stage_lhsT(np.ascontiguousarray(g("write_w").T), 512),
        "inp": stage_lhsT(np.ascontiguousarray(g("in_proj_w").T), 512),
        "ops": stage_lhsT(np.ascontiguousarray(g("out_proj_w").T), 512),
        "wb_in": g("write_b").reshape(1, 2048),
        "npr_in": g("norm_post_w").reshape(1, 2048),
        "wksp_in": np.ascontiguousarray(g("workspace")),
        "lffnv_in": vecs,
        "cbv_in": g("compete_b").reshape(16, 1),
    }


def l1a_percore(c):
    sq, ca, cb = core_chunks(c)
    bidx = attn_b_percore(c)["bidx_in"]
    mfac = np.zeros((128, 16), np.float32)
    fam = np.zeros((128, 2), np.float32)
    for t, i in ((0, ca), (1, cb)):
        if i == 0:
            fam[:, t] = 1.0
        for j in range(8):
            m = i - 1 - j
            mfac[:, t * 8 + j] = float(m) if m >= 0 else 1e4
    return {"bidx2_in": bidx, "mfac_in": mfac, "fam_in": fam}


def attn_b_percore(c):
    sq, ca, cb = core_chunks(c)
    bidx = np.zeros((128, 2), np.int32)
    for t, i in ((0, ca), (1, cb)):
        prev = i - 1
        if prev < 0:
            bidx[:, t] = 1024 + np.arange(128)
        else:
            base = own_chunk(prev) * 256 + tslot_chunk(prev) * 128
            bidx[:, t] = base + np.arange(128)
    return {"bidx_in": bidx}


def build_wmask():
    m = np.full((128, 5 * CH), -1e30, np.float32)
    p = np.arange(128)[:, None]
    f = np.arange(CH)[None, :]
    for kc in range(5):
        delta = f - (kc - 1) * 128 - p
        valid = (delta >= 0) & (delta <= 56) & (delta % 8 == 0)
        vals = -0.5 * np.log(np.where(valid, delta, 0) + 1.0)
        m[:, kc * CH:(kc + 1) * CH] = np.where(valid, vals, -1e30)
    return m.astype(BF)


def stage_l0_attn_a(wq, wk, wv, wo, g):
    sh = {}
    for wn, w in (("wq", wq), ("wk", wk), ("wv", wv), ("wo", wo)):
        sh[wn + "s"] = stage_lhsT(np.ascontiguousarray(np.asarray(w, np.float32).T), 256)
    sh["g_in"] = np.ascontiguousarray(np.asarray(g, np.float32).reshape(KD, 128).T)
    sh["wmask_in"] = build_wmask()
    return sh


def attn_a_percore(x0_full, c):
    sq, ca, cb = core_chunks(c)
    halo = np.zeros((256, D), np.float32)
    kill = np.zeros((128, 2), np.float32)
    for tt, idx in ((0, ca), (1, cb)):
        if idx == 0:
            kill[:, tt] = -1e30
        else:
            halo[tt * 128:(tt + 1) * 128] = x0_full[sq, idx * CH - 128: idx * CH]
    return {"x_halo": np.ascontiguousarray(halo.T), "kill_in": kill}


def stage_l0_ffn(W1, W2, W3):
    w1s = stage_lhsT(np.ascontiguousarray(np.asarray(W1, np.float32).T), 512)
    w3s = stage_lhsT(np.ascontiguousarray(np.asarray(W3, np.float32).T), 512)
    w2T = np.ascontiguousarray(np.asarray(W2, np.float32).T)
    w2s = np.ascontiguousarray(w2T[h2perm()])
    return {"w1s": w1s, "w3s": w3s, "w2s": w2s}


def run_blocks(blocks, x_full, weights, percore_fn=None):
    nc = build_program(blocks)
    xs = shard_x(x_full)
    cst = const_inputs()
    in_maps = []
    for c in range(NCORES):
        m = {"x_in": xs[c], "cid": np.array([[c]], np.int32), **cst, **weights}
        if percore_fn is not None:
            m.update(percore_fn(c))
        in_maps.append(m)
    res = run_bass_kernel_spmd(nc, in_maps, core_ids=list(range(NCORES)))
    global LAST_RESULTS
    LAST_RESULTS = res.results
    return unshard_x([r["x_out"] for r in res.results])


def kernel(**inputs):
    d = inputs
    x = np.asarray(d["L0_attn_a_x"], np.float32)
    w_aa = stage_l0_attn_a(d["L0_attn_a_wq"], d["L0_attn_a_wk"], d["L0_attn_a_wv"],
                           d["L0_attn_a_wo"], d["L0_attn_a_g"])
    w_ff = stage_l0_ffn(d["L0_ffn_W1"], d["L0_ffn_W2"], d["L0_ffn_W3"])
    w_ab = stage_l0_attn_b(d["L0_attn_b_W_in"], d["L0_attn_b_conv_w"],
                           d["L0_attn_b_conv_b"], d["L0_attn_b_A_log"],
                           d["L0_attn_b_D"], d["L0_attn_b_dt_bias"],
                           d["L0_attn_b_norm_w"], d["L0_attn_b_W_out"],
                           d["L0_attn_b_mup_vector"])
    w_ra = stage_l1_attn_a(d)
    w_lf = stage_l1_ffn(d)
    w_cb = stage_l1_attn_b(d)
    x0 = x
    x = run_blocks(["l0_attn_a"], x, w_aa, percore_fn=lambda c: attn_a_percore(x0, c))
    x = run_blocks(["l0_ffn"], x, w_ff)
    x = run_blocks(["l0_attn_b"], x, w_ab, percore_fn=lambda c: attn_b_percore(c))
    x = run_blocks(["l0_ffn"], x, w_ff)
    x = run_blocks(["l1_attn_a"], x, w_ra, percore_fn=lambda c: l1a_percore(c))
    x = run_blocks(["l1_ffn"], x, w_lf)
    x = run_blocks(["l1_attn_b"], x, w_cb, percore_fn=lambda c: cattn_percore(c))
    x = run_blocks(["l1_ffn"], x, w_lf)
    return x
